# version 16
# baseline (speedup 1.0000x reference)
import numpy as np
import ml_dtypes
import concourse.bass as bass
import concourse.mybir as mybir
from concourse.bass_utils import run_bass_kernel_spmd

F32 = mybir.dt.float32
BF16 = mybir.dt.bfloat16
AF = mybir.ActivationFunctionType
ALU = mybir.AluOpType
AX = mybir.AxisListType

NCORES = 8
D = 2048
TPC = 1024
NT = 8
ND = 16
EPS = 1e-5
SQD = float(np.sqrt(D))

_PHASES = 99
_DEBUG = False
_M2N = 8

BASE = 16512
OFF_X = BASE
OFF_HT = BASE + 65536
OFF_A = BASE + 98304
OFF_W = BASE + 163840
OFF_C = BASE + 208896
A_RING = OFF_A + 40960
A_TMP = OFF_A + 53248
A_BC = OFF_A + 57344
A_SIG = OFF_A + 59392
A_SMALL = OFF_A + 61440
W_SCR = OFF_W + 16384
C_IDB = OFF_C
C_CONVW = OFF_C + 256
C_CONVB = OFF_C + 768
C_ROUT = OFF_C + 896
C_ONES = OFF_C + 1184
C_IDF = OFF_C + 1440
C_STAT = OFF_C + 1952
C_HALOT = OFF_C + 2336


class Ctr:
    def __init__(self, nc, name):
        self.sem = nc.alloc_semaphore(name)
        self.n = 0

    def inc(self, inst, by=1):
        inst.then_inc(self.sem, by)
        self.n += by
        return self.n


class InstProxy:
    __slots__ = ("post",)

    def __init__(self):
        self.post = []

    def then_inc(self, sem, by):
        self.post.append((sem, by))
        return self


class EngProxy:
    def __init__(self):
        self.q = []

    def __getattr__(self, attr):
        def call(*a, **k):
            res = InstProxy()
            self.q.append((attr, a, k, res))
            return res
        return call


class BankPool:

    def __init__(self, pb):
        self.pb = pb
        self.guard = [None] * len(pb)
        self.idx = 0

    def acquire(self, te, n=1):
        out = []
        for _ in range(n):
            i = self.idx
            self.idx = (self.idx + 1) % len(self.pb)
            g = self.guard[i]
            if g is not None and g[1] > 0:
                te.wait_ge(g[0].sem, g[1])
            self.guard[i] = None
            out.append(i)
        return out

    def release(self, i, ctr):
        self.guard[i] = (ctr, ctr.n)


def _pack(inputs):
    f = np.ascontiguousarray
    W_in = np.asarray(inputs["L0_attn_W_in"], np.float32)
    conv_w = np.asarray(inputs["L0_attn_conv_w"], np.float32)
    conv_b = np.asarray(inputs["L0_attn_conv_b"], np.float32)
    W_out = np.asarray(inputs["L0_attn_W_out"], np.float32)
    W_m1 = np.asarray(inputs["L0_attn_W_mlp1"], np.float32)
    W_m2 = np.asarray(inputs["L0_attn_W_mlp2"], np.float32)
    W_r = np.asarray(inputs["L0_ffn_W_router"], np.float32)
    W_sg = np.asarray(inputs["L0_ffn_W_sgate"], np.float32)
    Wsgu = np.asarray(inputs["L0_ffn_Ws_gate_up"], np.float32)
    Wsd = np.asarray(inputs["L0_ffn_Ws_down"], np.float32)
    Wegu = np.asarray(inputs["L0_ffn_We_gate_up"], np.float32)
    Wed = np.asarray(inputs["L0_ffn_We_down"], np.float32)

    b = {}
    t = W_in.reshape(64, 128, 16, 128).transpose(0, 3, 2, 1)
    seq = []
    for c in range(32):
        seq += [c, c + 32]
    b["win"] = f(t[seq].reshape(64, 128, 2048))
    b["wout"] = f(W_out.T.reshape(32, 128, 8, 256).transpose(2, 0, 1, 3))
    b["m1"] = f(W_m1.reshape(64, 128, 16, 128).transpose(0, 3, 2, 1)
                .reshape(64, 128, 2048))
    b["m2"] = f(W_m2.T.reshape(8, 8, 128, 2048))
    seqe = []
    for c in range(11):
        seqe += [c, c + 11]
    t = Wegu.reshape(8, 16, 128, 22, 128).transpose(0, 3, 2, 1, 4)
    b["egu"] = f(t[:, seqe].reshape(8, 22, 128, 2048))
    b["ed"] = f(Wed.reshape(8, 11, 128, 2048))
    t = Wsgu.reshape(16, 128, 88, 128).transpose(2, 1, 0, 3)
    sq = []
    for q in range(4):
        for j in range(11):
            sq += [11 * q + j, 44 + 11 * q + j]
    b["sgu"] = f(t[np.array(sq)].reshape(4, 22, 128, 2048))
    b["sd"] = f(Wsd.reshape(4, 11, 128, 2048))
    b["convw"] = f(conv_w.reshape(32, 128, 4).transpose(1, 0, 2).reshape(128, 128))
    b["convb"] = f(conv_b.reshape(32, 128).T)
    rout = np.zeros((128, 144), np.float32)
    for dc in range(16):
        rout[:, dc * 9:dc * 9 + 8] = W_r[:, dc * 128:(dc + 1) * 128].T
        rout[:, dc * 9 + 8] = W_sg[0, dc * 128:(dc + 1) * 128]
    b["rout"] = f(rout)
    b["idf"] = np.eye(128, dtype=np.float32)
    b["idb"] = np.eye(128, dtype=np.float32).astype(ml_dtypes.bfloat16)
    b["ones"] = np.ones((1, 128), ml_dtypes.bfloat16)
    return b


def build_program():
    nc = bass.Bass("TRN2", target_bir_lowering=False)

    xin = nc.dram_tensor("xc", [TPC, D], F32, kind="ExternalInput").ap()
    xhal = nc.dram_tensor("xh", [3, D], F32, kind="ExternalInput").ap()
    d_win = nc.dram_tensor("b_win", [64, 128, 2048], F32, kind="ExternalInput").ap()
    d_wout = nc.dram_tensor("b_wout", [8, 32, 128, 256], F32, kind="ExternalInput").ap()
    d_m1 = nc.dram_tensor("b_m1", [64, 128, 2048], F32, kind="ExternalInput").ap()
    d_m2 = nc.dram_tensor("b_m2", [8, 8, 128, 2048], F32, kind="ExternalInput").ap()
    d_egu = nc.dram_tensor("b_egu", [8, 22, 128, 2048], F32, kind="ExternalInput").ap()
    d_ed = nc.dram_tensor("b_ed", [8, 11, 128, 2048], F32, kind="ExternalInput").ap()
    d_sgu = nc.dram_tensor("b_sgu", [4, 22, 128, 2048], F32, kind="ExternalInput").ap()
    d_sd = nc.dram_tensor("b_sd", [4, 11, 128, 2048], F32, kind="ExternalInput").ap()
    d_convw = nc.dram_tensor("b_convw", [128, 128], F32, kind="ExternalInput").ap()
    d_convb = nc.dram_tensor("b_convb", [128, 32], F32, kind="ExternalInput").ap()
    d_rout = nc.dram_tensor("b_rout", [128, 144], F32, kind="ExternalInput").ap()
    d_idf = nc.dram_tensor("b_idf", [128, 128], F32, kind="ExternalInput").ap()
    d_idb = nc.dram_tensor("b_idb", [128, 128], BF16, kind="ExternalInput").ap()
    d_ones = nc.dram_tensor("b_ones", [1, 128], BF16, kind="ExternalInput").ap()
    out = nc.dram_tensor("out", [TPC, D], F32, kind="ExternalOutput").ap()
    dbg = {}
    if _DEBUG:
        dbg["hT"] = nc.dram_tensor("dbg_hT", [128, ND * 1024], BF16, kind="ExternalOutput").ap()
        dbg["haloT"] = nc.dram_tensor("dbg_haloT", [128, 48], BF16, kind="ExternalOutput").ap()
        dbg["y"] = nc.dram_tensor("dbg_y", [128, 32 * 1024], BF16, kind="ExternalOutput").ap()
        dbg["xp"] = nc.dram_tensor("dbg_xp", [128, 1027], F32, kind="ExternalOutput").ap()
        dbg["sxc"] = nc.dram_tensor("dbg_sxc", [128, 2048], BF16, kind="ExternalOutput").ap()
        dbg["woutb"] = nc.dram_tensor("dbg_woutb", [128, 16384], BF16, kind="ExternalOutput").ap()

    def at(name, off, shape, dt):
        return nc.alloc_sbuf_tensor_at(name, shape, dt, offset=off).ap()

    x_res = at("x_res", OFF_X, [128, NT * 2048], F32)
    hT = at("hT", OFF_HT, [128, ND * 1024], BF16)
    regA = at("regA", OFF_A, [128, 32 * 1024], BF16)
    coT = at("coT", OFF_A + 22528, [1, 9 * 1024], BF16)
    ringF = [at(f"ringF{i}", A_RING + i * 4096, [128, 2048], BF16) for i in range(3)]
    tmpA = at("tmpA", A_TMP, [128, 1024], F32)
    bc_sb = at("bc_sb", A_BC, [128, 1024], BF16)
    sig_sb = at("sig_sb", A_SIG, [128, 1024], BF16)
    smallA = at("smallA", A_SMALL, [128, 1024], F32)
    wbuf = at("wbuf", OFF_W, [128, 22528], BF16)
    ringW = [at(f"ringW{i}", OFF_W + i * 8192, [128, 4096], BF16) for i in range(2)]
    woutb = [at(f"woutb{i}", OFF_W + i * 16384, [128, 8192], BF16) for i in range(2)]
    xh_sb = at("xh_sb", W_SCR, [3, 2048], F32)
    xp_sb = at("xp_sb", W_SCR, [128, 1027], F32)
    tmp2 = at("tmp2", W_SCR + 4128, [128, 1024], F32)
    tmp3 = at("tmp3", W_SCR + 8224, [128, 1024], F32)
    sxc = at("sxc", W_SCR + 12320, [128, 2 * 1024], BF16)
    idb = at("idb", C_IDB, [128, 128], BF16)
    convw = at("convw", C_CONVW, [128, 128], F32)
    convb = at("convb", C_CONVB, [128, 32], F32)
    routw = at("routw", C_ROUT, [128, 144], BF16)
    ones = at("ones", C_ONES, [1, 128], BF16)
    idf = at("idf", C_IDF, [128, 128], F32)
    stat = at("stat", C_STAT, [128, 96], F32)
    haloT = at("haloT", C_HALOT, [128, 48], BF16)

    pb = [nc.alloc_psum_tensor(f"pb{i}", [128, 512], F32) for i in range(8)]

    s_x = Ctr(nc, "s_x")
    s_w = [Ctr(nc, "s_w0"), Ctr(nc, "s_w1")]
    s_f = [Ctr(nc, "s_f0"), Ctr(nc, "s_f1"), Ctr(nc, "s_f2")]
    s_r = Ctr(nc, "s_r")
    s_xp = [Ctr(nc, "s_xp0"), Ctr(nc, "s_xp1")]
    s_pe = Ctr(nc, "s_pe")
    s_dve = Ctr(nc, "s_dve")
    s_act = Ctr(nc, "s_act")
    s_st = Ctr(nc, "s_st")

    sy, gp, te, ve, sc = EngProxy(), EngProxy(), EngProxy(), EngProxy(), EngProxy()
    pool = BankPool(pb)

    hT_r = hT[:].rearrange("p (dc t) -> p dc t", t=1024)

    def ht_chunk(dc, g=None):
        if g is None:
            return hT[:, dc * 1024:(dc + 1) * 1024]
        return hT[:, dc * 1024 + g * 512: dc * 1024 + (g + 1) * 512]

    def xt(ti):
        return x_res[:, ti * 2048:(ti + 1) * 2048]

    def ya(c):
        return regA[:, c * 1024:(c + 1) * 1024]

    def dve_self(v):
        v.wait_ge(s_dve.sem, s_dve.n)

    def V(op, *a, **k):
        dve_self(ve)
        s_dve.inc(getattr(ve, op)(*a, **k))

    def S(op, *a, **k):
        sc.wait_ge(s_act.sem, s_act.n)
        sc.wait_ge(s_dve.sem, s_dve.n)
        s_act.inc(getattr(sc, op)(*a, **k))

    def batch_stats(first_wait=None, tile_waits=None):
        for ti in range(NT):
            if first_wait is not None and ti == 0:
                ve.wait_ge(first_wait[0].sem, first_wait[1])
            if tile_waits is not None:
                ve.wait_ge(tile_waits[ti][0].sem, tile_waits[ti][1])
            dve_self(ve)
            for sgi in range(4):
                i = ve.bn_stats(stat[:, sgi * 6:sgi * 6 + 6],
                                xt(ti)[:, sgi * 512:(sgi + 1) * 512])
            s_dve.inc(i)
            dve_self(ve)
            s_dve.inc(ve.bn_aggr(stat[:, 24 + ti:24 + ti + 9:8], stat[:, 0:24]))

    def tile_stats(ti):
        dve_self(ve)
        for sgi in range(4):
            i = ve.bn_stats(stat[:, sgi * 6:sgi * 6 + 6],
                            xt(ti)[:, sgi * 512:(sgi + 1) * 512])
        s_dve.inc(i)
        dve_self(ve)
        s_dve.inc(ve.bn_aggr(stat[:, 24 + ti:24 + ti + 9:8], stat[:, 0:24]))

    def _apply_tile(kind, ti, stage, stage_guard):
        dve_self(ve)
        if stage_guard[ti % 2] > 0:
            ve.wait_ge(s_pe.sem, stage_guard[ti % 2])
        if kind == "ln_u":
            s_dve.inc(ve.tensor_scalar(
                xt(ti), xt(ti), stat[:, 64 + ti:65 + ti],
                stat[:, 72 + ti:73 + ti], ALU.mult, ALU.add))
            dve_self(ve)
            s_dve.inc(ve.tensor_scalar(
                stage[ti % 2], xt(ti), stat[:, 80 + ti:81 + ti], None,
                ALU.mult))
        else:
            s_dve.inc(ve.tensor_scalar(
                stage[ti % 2], xt(ti), stat[:, 64 + ti:65 + ti],
                stat[:, 72 + ti:73 + ti], ALU.mult, ALU.add))
        happly = s_dve.n
        for g4 in range(4):
            bk = pool.acquire(te, 1)[0]
            if g4 == 0:
                te.wait_ge(s_dve.sem, happly)
            pv = pb[bk].ap().bitcast(BF16)
            for j in range(4):
                dc = g4 * 4 + j
                i = te.transpose(pv[:, j * 128:(j + 1) * 128],
                                 stage[ti % 2][:, dc * 128:(dc + 1) * 128],
                                 idb[:])
            s_pe.inc(i)
            stage_guard[ti % 2] = s_pe.n
            sc.wait_ge(s_pe.sem, s_pe.n)
            dst = hT_r[:, 4 * g4:4 * g4 + 4, ti * 128:(ti + 1) * 128]
            srcv = pv[:, 0:512].rearrange("p (c t) -> p c t", t=128)
            i = sc.copy(dst, srcv)
            s_act.inc(i)
            pool.release(bk, s_act)

    def batch_chain(kind, lo=0, hi=8):
        m8 = stat[:, 24 + lo:24 + hi]
        v8 = stat[:, 32 + lo:32 + hi]
        c0 = stat[:, 40 + lo:40 + hi]
        c1 = stat[:, 48 + lo:48 + hi]
        c2 = stat[:, 56 + lo:56 + hi]
        s8 = stat[:, 64 + lo:64 + hi]
        b8 = stat[:, 72 + lo:72 + hi]
        r8 = stat[:, 80 + lo:80 + hi]
        if kind == "ln_u":
            V("tensor_scalar", c0, v8, EPS, None, ALU.add)
            V("tensor_scalar", c1, v8, float(D), 1e-30, ALU.mult, ALU.max)
            S("activation", c0, c0, AF.Ln)
            S("activation", c0, c0, AF.Exp, scale=-0.5)
            S("activation", c1, c1, AF.Sqrt)
            ve.wait_ge(s_act.sem, s_act.n)
            V("tensor_tensor", c2, c1, c0, ALU.mult)
            V("tensor_scalar", c2, c2, 1e-12, None, ALU.max)
            V("reciprocal", c2, c2)
            V("tensor_scalar", r8, c2, SQD, None, ALU.mult)
            V("tensor_copy", s8, c0)
            V("tensor_tensor", b8, m8, c0, ALU.mult)
            V("tensor_scalar", b8, b8, -1.0, None, ALU.mult)
        elif kind == "rms":
            V("tensor_tensor", c0, m8, m8, ALU.mult)
            V("tensor_tensor", c0, c0, v8, ALU.add)
            V("tensor_scalar", c0, c0, float(D), 1e-30, ALU.mult, ALU.max)
            S("activation", c2, c0, AF.Sqrt)
            ve.wait_ge(s_act.sem, s_act.n)
            V("tensor_scalar", c2, c2, 1e-12, None, ALU.max)
            V("reciprocal", s8, c2)
            V("tensor_scalar", s8, s8, SQD, None, ALU.mult)
            V("tensor_scalar", b8, s8, 0.0, None, ALU.mult)
        else:
            V("tensor_scalar", c0, v8, EPS, None, ALU.add)
            S("activation", c1, c0, AF.Ln)
            S("activation", s8, c1, AF.Exp, scale=-0.5)
            ve.wait_ge(s_act.sem, s_act.n)
            V("tensor_tensor", b8, m8, s8, ALU.mult)
            V("tensor_scalar", b8, b8, -1.0, None, ALU.mult)

    def norm_tile(kind, src_ap, s_col, b_col, npart=128):
        dve_self(ve)
        for sgi in range(4):
            i = ve.bn_stats(stat[0:npart, sgi * 6:sgi * 6 + 6],
                            src_ap[0:npart, sgi * 512:(sgi + 1) * 512])
        s_dve.inc(i)
        V("bn_aggr", stat[0:npart, 24:26], stat[0:npart, 0:24])
        m = stat[0:npart, 24:25]
        var = stat[0:npart, 25:26]
        c0 = stat[0:npart, 42:43]
        c1 = stat[0:npart, 43:44]
        c2 = stat[0:npart, 44:45]
        sA = stat[0:npart, s_col:s_col + 1]
        bA = stat[0:npart, b_col:b_col + 1]
        if kind == "ln":
            V("tensor_scalar", c0, var, EPS, None, ALU.add)
            S("activation", c1, c0, AF.Ln)
            S("activation", sA, c1, AF.Exp, scale=-0.5)
            ve.wait_ge(s_act.sem, s_act.n)
            V("tensor_tensor", bA, m, sA, ALU.mult)
            V("tensor_scalar", bA, bA, -1.0, None, ALU.mult)
        elif kind == "rms":
            V("tensor_tensor", c0, m, m, ALU.mult)
            V("tensor_tensor", c0, c0, var, ALU.add)
            V("tensor_scalar", c0, c0, float(D), 1e-30, ALU.mult, ALU.max)
            S("activation", c2, c0, AF.Sqrt)
            ve.wait_ge(s_act.sem, s_act.n)
            V("tensor_scalar", c2, c2, 1e-12, None, ALU.max)
            V("reciprocal", sA, c2)
            V("tensor_scalar", sA, sA, SQD, None, ALU.mult)
            V("tensor_scalar", bA, sA, 0.0, None, ALU.mult)
        elif kind == "ln_u":
            rA = stat[0:npart, 35:36]
            V("tensor_scalar", c0, var, EPS, None, ALU.add)
            V("tensor_scalar", c1, var, float(D), 1e-30, ALU.mult, ALU.max)
            S("activation", c0, c0, AF.Ln)
            S("activation", c0, c0, AF.Exp, scale=-0.5)
            S("activation", c1, c1, AF.Sqrt)
            ve.wait_ge(s_act.sem, s_act.n)
            V("tensor_tensor", c2, c1, c0, ALU.mult)
            V("tensor_scalar", c2, c2, 1e-12, None, ALU.max)
            V("reciprocal", c2, c2)
            V("tensor_scalar", rA, c2, SQD, None, ALU.mult)
            V("tensor_copy", sA, c0)
            V("tensor_tensor", bA, m, c0, ALU.mult)
            V("tensor_scalar", bA, bA, -1.0, None, ALU.mult)
        else:
            V("tensor_scalar", c0, var, EPS, None, ALU.add)
            V("tensor_scalar", c1, var, float(D), 1e-30, ALU.mult, ALU.max)
            S("activation", c0, c0, AF.Ln)
            S("activation", c0, c0, AF.Exp, scale=-0.5)
            S("activation", c1, c1, AF.Sqrt)
            ve.wait_ge(s_act.sem, s_act.n)
            V("tensor_tensor", c2, c1, c0, ALU.mult)
            V("tensor_scalar", c2, c2, 1e-12, None, ALU.max)
            V("reciprocal", c2, c2)
            V("tensor_scalar", c2, c2, SQD, None, ALU.mult)
            V("tensor_tensor", sA, c0, c2, ALU.mult)
            V("tensor_tensor", bA, m, sA, ALU.mult)
            V("tensor_scalar", bA, bA, -1.0, None, ALU.mult)

    def build_hT(kind, first_wait=None, stats_done=False, tile_waits=None,
                 stage_off=0, chain_done=False):
        def _emit_half(kind, tis):
            for ti in tis:
                _apply_tile(kind, ti, stage, stage_guard)

        """Norm + transpose x_res -> hT (all 8 tiles). Uses regA[0:4096] as
        bf16 staging (2 alternating tiles). kind 'ln_u' also rewrites
        x_res tile in place to ln(x)."""
        stage = [regA[:, stage_off:stage_off + 2048],
                 regA[:, stage_off + 2048:stage_off + 4096]]
        stage_guard = [0, 0]
        if not stats_done:
            for ti in range(NT):
                if first_wait is not None and ti == 0:
                    ve.wait_ge(first_wait[0].sem, first_wait[1])
                if tile_waits is not None:
                    ve.wait_ge(tile_waits[ti][0].sem, tile_waits[ti][1])
                dve_self(ve)
                for sgi in range(4):
                    i = ve.bn_stats(stat[:, sgi * 6:sgi * 6 + 6],
                                    xt(ti)[:, sgi * 512:(sgi + 1) * 512])
                s_dve.inc(i)
                dve_self(ve)
                s_dve.inc(ve.bn_aggr(stat[:, 24 + ti:24 + ti + 9:8],
                                     stat[:, 0:24]))
                if ti == 3:
                    batch_chain(kind, 0, 4)
                    _emit_half(kind, range(0, 4))
                elif ti == 7:
                    batch_chain(kind, 4, 8)
                    _emit_half(kind, range(4, 8))
            return
        if not chain_done:
            batch_chain(kind)
        for ti in range(NT):
            _apply_tile(kind, ti, stage, stage_guard)

    for src, dst in ((d_convw, convw[:]), (d_convb, convb[:]),
                     (d_idf, idf[:]), (d_idb, idb[:]), (d_ones, ones[:])):
        i = sy.dma_start(out=dst, in_=src)
        s_x.inc(i, 16)
    i = sy.dma_start(out=xh_sb[:], in_=xhal)
    s_x.inc(i, 16)
    consts_cnt = s_x.n
    xp_sems = [s_xp[0], s_xp[1], s_f[0], s_f[1]]
    x_piece = []
    for h in range(4):
        i = sy.dma_start(
            out=x_res[:, h * 4096:(h + 1) * 4096].rearrange(
                "p (i n) -> p i n", n=2048),
            in_=xin[h * 256:(h + 1) * 256, :].rearrange("(i p) n -> p i n", p=128))
        xp_sems[h].inc(i, 16)
        x_piece += [(xp_sems[h], xp_sems[h].n)] * 2

    i = gp.dma_start(out=routw[:], in_=d_rout)
    s_r.inc(i, 16)

    ve.wait_ge(s_x.sem, consts_cnt)
    norm_tile("lnrms", xh_sb, 26, 34, npart=3)
    hstage = smallA[:].bitcast(BF16)
    dve_self(ve)
    s_dve.inc(ve.tensor_scalar(hstage[0:3, 0:2048], xh_sb[:], stat[0:3, 26:27],
                               stat[0:3, 34:35], ALU.mult, ALU.add))
    bk = pool.acquire(te, 1)[0]
    te.wait_ge(s_dve.sem, s_dve.n)
    pv = pb[bk].ap().bitcast(BF16)
    for dc in range(ND):
        i = te.transpose(pv[:, dc * 4:dc * 4 + 3],
                         hstage[0:3, dc * 128:(dc + 1) * 128], idb[0:3, 0:3])
    s_pe.inc(i)
    sc.wait_ge(s_pe.sem, s_pe.n)
    i = sc.copy(haloT[:].rearrange("p (c t) -> p c t", t=3),
                       pv[:, 0:64].rearrange("p (c t) -> p c t", t=4)[:, :, 0:3])
    s_act.inc(i)
    pool.release(bk, s_act)
    build_hT("ln_u", tile_waits=x_piece)
    hT1_done = (s_act, s_act.n)

    dbg_srcs = {"hT": hT[:], "haloT": haloT[:], "y": regA[:], "xp": xp_sb[:], "sxc": sxc[:],
                "woutb": wbuf[:, 0:16384]}
    if _PHASES <= 0:
        _epilogue(sy, s_pe, s_dve, s_act, s_st, out, x_res, dbg, dbg_srcs)
        return _finish(nc, sy, gp, te, ve, sc)

    ringW_guard = [0, 0]
    convdone = 0
    tmpA_free = 0
    sxc_guard = [0, 0]
    y_done = []
    for p in range(32):
        sl = p % 2
        if ringW_guard[sl] > 0:
            gp.wait_ge(s_pe.sem, ringW_guard[sl])
        i = gp.dma_start(out=ringW[sl][:].rearrange("p (c n) -> p c n", c=2),
                         in_=d_win.rearrange("(g c) p n -> g p c n", c=2)[p])
        s_w[sl].inc(i, 16)
        te.wait_ge(s_w[sl].sem, s_w[sl].n)
        if p == 0:
            te.wait_ge(hT1_done[0].sem, hT1_done[1])
        bxp = pool.acquire(te, 2)
        bh = pool.acquire(te, 1)[0]
        for g in range(2):
            for dc in range(ND):
                st0, sp0 = dc == 0, dc == ND - 1
                lhs = ringW[sl][:, dc * 128:(dc + 1) * 128]
                i = te.matmul(pb[bxp[g]].ap()[:, 0:512], lhs, ht_chunk(dc, g),
                              start=st0, stop=sp0)
        for dc in range(ND):
            st0, sp0 = dc == 0, dc == ND - 1
            lhs = ringW[sl][:, dc * 128:(dc + 1) * 128]
            i = te.matmul(pb[bh].ap()[:, 0:3], lhs, haloT[:, dc * 3:(dc + 1) * 3],
                          start=st0, stop=sp0)
        s_pe.inc(i)
        xp_mm = s_pe.n
        bz = pool.acquire(te, 2)
        for g in range(2):
            for dc in range(ND):
                st0, sp0 = dc == 0, dc == ND - 1
                lhs = ringW[sl][:, 2048 + dc * 128:2048 + (dc + 1) * 128]
                i = te.matmul(pb[bz[g]].ap()[:, 0:512], lhs, ht_chunk(dc, g),
                              start=st0, stop=sp0)
        s_pe.inc(i)
        z_mm = s_pe.n
        ringW_guard[sl] = z_mm
        sc.wait_ge(s_pe.sem, xp_mm)
        if convdone > 0:
            sc.wait_ge(s_dve.sem, convdone)
        sc.copy(xp_sb[:, 3:515], pb[bxp[0]].ap()[:, 0:512])
        sc.copy(xp_sb[:, 515:1027], pb[bxp[1]].ap()[:, 0:512])
        i = sc.copy(xp_sb[:, 0:3], pb[bh].ap()[:, 0:3])
        s_act.inc(i)
        xp_evac = s_act.n
        pool.release(bxp[0], s_act)
        pool.release(bxp[1], s_act)
        pool.release(bh, s_act)
        ve.wait_ge(s_act.sem, xp_evac)
        ve.tensor_scalar(tmp2[:], xp_sb[:, 0:1024], convw[:, 4 * p:4 * p + 1],
                         convb[:, p:p + 1], ALU.mult, ALU.add)
        for k in (1, 2, 3):
            ve.tensor_scalar(tmp3[:], xp_sb[:, k:k + 1024],
                             convw[:, 4 * p + k:4 * p + k + 1], None, ALU.mult)
            i = ve.tensor_tensor(tmp2[:], tmp2[:], tmp3[:], ALU.add)
        s_dve.inc(i)
        convdone = s_dve.n
        sc.wait_ge(s_dve.sem, convdone)
        if sxc_guard[sl] > 0:
            sc.wait_ge(s_dve.sem, sxc_guard[sl])
        i = sc.activation(sxc[:, sl * 1024:(sl + 1) * 1024], tmp2[:], AF.Silu)
        s_act.inc(i)
        sxc_done = s_act.n
        sc.wait_ge(s_pe.sem, z_mm)
        if tmpA_free > 0:
            sc.wait_ge(s_dve.sem, tmpA_free)
        sc.activation(tmp3[:, 0:512], pb[bz[0]].ap()[:, 0:512], AF.Silu)
        i = sc.activation(tmp3[:, 512:1024], pb[bz[1]].ap()[:, 0:512], AF.Silu)
        s_act.inc(i)
        z_silu = s_act.n
        pool.release(bz[0], s_act)
        pool.release(bz[1], s_act)
        ve.wait_ge(s_act.sem, z_silu)
        i = ve.tensor_tensor(ya(p), tmp3[:], sxc[:, sl * 1024:(sl + 1) * 1024],
                             ALU.mult)
        s_dve.inc(i)
        tmpA_free = s_dve.n
        sxc_guard[sl] = s_dve.n
        y_done.append(s_dve.n)

    woutb_guard = [0, 0]
    m2_tile_done = [0] * NT
    for de in range(_M2N):
        sl = de % 2
        if de == 1:
            gp.wait_ge(s_dve.sem, y_done[31])
        if de == 0:
            gp.wait_ge(s_pe.sem, ringW_guard[0])
            i = gp.dma_start(
                out=woutb[0][:, 0:4096].rearrange("p (c n) -> p c n", n=256),
                in_=d_wout[0, 0:16].rearrange("c p n -> p c n"))
            s_w[0].inc(i, 16)
            gp.wait_ge(s_pe.sem, ringW_guard[1])
            i = gp.dma_start(
                out=woutb[0][:, 4096:8192].rearrange("p (c n) -> p c n", n=256),
                in_=d_wout[0, 16:32].rearrange("c p n -> p c n"))
            s_w[0].inc(i, 16)
        else:
            if woutb_guard[sl] > 0:
                gp.wait_ge(s_pe.sem, woutb_guard[sl])
            i = gp.dma_start(out=woutb[sl][:].rearrange("p (c n) -> p c n", n=256),
                             in_=d_wout[de].rearrange("c p n -> p c n"))
            s_w[sl].inc(i, 16)
        te.wait_ge(s_w[sl].sem, s_w[sl].n)
        if de == 0:
            te.wait_ge(s_dve.sem, y_done[31])
        for tt in range(NT):
            bk = pool.acquire(te, 1)[0]
            for ec in range(32):
                i = te.matmul(pb[bk].ap()[:, 0:256],
                              ya(ec)[:, tt * 128:(tt + 1) * 128],
                              woutb[sl][:, ec * 256:(ec + 1) * 256],
                              start=(ec == 0), stop=(ec == 31))
            s_pe.inc(i)
            ve.wait_ge(s_pe.sem, s_pe.n)
            xs = x_res[:, tt * 2048 + de * 256: tt * 2048 + (de + 1) * 256]
            i = ve.tensor_tensor(xs, pb[bk].ap()[:, 0:256], xs, ALU.add)
            s_dve.inc(i)
            pool.release(bk, s_dve)
            m2_tile_done[tt] = s_dve.n
            if de == 7 and _PHASES > 1:
                tile_stats(tt)
        woutb_guard[sl] = s_pe.n
    m2_last_mm = s_pe.n

    if _PHASES <= 1:
        _epilogue(sy, s_pe, s_dve, s_act, s_st, out, x_res, dbg, dbg_srcs)
        return _finish(nc, sy, gp, te, ve, sc)

    build_hT("rms", stats_done=True)

    ringF_guard = [m2_last_mm] * 3
    m5_apply = [0] * NT
    mlp2_guard = 0
    m4_tile_done = [0] * NT
    for q in range(8):
        for ec in range(8):
            gec = 8 * q + ec
            slf = gec % 3
            if ringF_guard[slf] > 0:
                gp.wait_ge(s_pe.sem, ringF_guard[slf])
            i = gp.dma_start(out=ringF[slf][:], in_=d_m1[gec])
            s_f[slf].inc(i, 16)
            if ec in (3, 5, 6, 7):
                k = (3, 5, 6, 7).index(ec)
                if k == 0:
                    gp.wait_ge(s_pe.sem, mlp2_guard if q > 0 else m2_last_mm)
                i = gp.dma_start(
                    out=wbuf[:, k * 4096:(k + 1) * 4096].rearrange(
                        "p (c n) -> p c n", n=2048),
                    in_=d_m2[q, 2 * k:2 * k + 2].rearrange("c p n -> p c n"))
                s_r.inc(i, 16)
            te.wait_ge(s_f[slf].sem, s_f[slf].n)
            if q == 0 and ec == 0:
                te.wait_ge(s_act.sem, s_act.n)
            bks = pool.acquire(te, 2)
            for g in range(2):
                for dc in range(ND):
                    st0, sp0 = dc == 0, dc == ND - 1
                    lhs = ringF[slf][:, dc * 128:(dc + 1) * 128]
                    i = te.matmul(pb[bks[g]].ap()[:, 0:512], lhs, ht_chunk(dc, g),
                                  start=st0, stop=sp0)
            s_pe.inc(i)
            ringF_guard[slf] = s_pe.n
            sc.wait_ge(s_pe.sem, s_pe.n)
            if mlp2_guard > 0 and ec == 0:
                sc.wait_ge(s_pe.sem, mlp2_guard)
            sc.activation(ya(ec)[:, 0:512], pb[bks[0]].ap()[:, 0:512], AF.Relu)
            i = sc.activation(ya(ec)[:, 512:1024], pb[bks[1]].ap()[:, 0:512], AF.Relu)
            s_act.inc(i)
            pool.release(bks[0], s_act)
            pool.release(bks[1], s_act)
        relu_all = s_act.n
        if q == 7:
            h2T_free = s_pe.n
            xstg = [hT[:, i * 4096:(i + 1) * 4096].bitcast(F32) for i in range(2)]
            v_tile_done = [0] * NT
            sy.wait_ge(s_pe.sem, h2T_free)
            for ti in range(2):
                i = sy.dma_start(out=xstg[ti % 2][:, 0:2048],
                                 in_=xin[ti * 128:(ti + 1) * 128, :])
                s_x.inc(i, 16)
                xarr = s_x.n
        te.wait_ge(s_r.sem, s_r.n)
        te.wait_ge(s_act.sem, relu_all)
        for tt in range(NT):
            for dh in range(2):
                bks = pool.acquire(te, 2)
                for hf in range(2):
                    for ec in range(8):
                        st0, sp0 = ec == 0, ec == 7
                        lhs = ya(ec)[:, tt * 128:(tt + 1) * 128]
                        off = ec * 2048 + dh * 1024 + hf * 512
                        i = te.matmul(pb[bks[hf]].ap()[:, 0:512], lhs,
                                      wbuf[:, off:off + 512], start=st0, stop=sp0)
                s_pe.inc(i)
                ve.wait_ge(s_pe.sem, s_pe.n)
                base = tt * 2048 + dh * 1024
                xs0 = x_res[:, base:base + 512]
                xs1 = x_res[:, base + 512:base + 1024]
                ve.tensor_tensor(xs0, pb[bks[0]].ap()[:, 0:512], xs0, ALU.add)
                i = ve.tensor_tensor(xs1, pb[bks[1]].ap()[:, 0:512], xs1, ALU.add)
                s_dve.inc(i)
                pool.release(bks[0], s_dve)
                pool.release(bks[1], s_dve)
                m4_tile_done[tt] = s_dve.n
                if q == 7 and dh == 1 and _PHASES > 2:
                    ve.wait_ge(s_x.sem, xarr)
                    i = ve.tensor_tensor(xt(tt), xstg[tt % 2][:, 0:2048],
                                         xt(tt), ALU.add)
                    s_dve.inc(i)
                    v_tile_done[tt] = s_dve.n
                    tile_stats(tt)
                    if tt == 3 and _PHASES > 2:
                        batch_chain("ln", 0, 4)
                        for ti in range(4):
                            dve_self(ve)
                            s_dve.inc(ve.tensor_scalar(
                                regA[:, 11264 + ti * 2048:11264 + (ti + 1) * 2048],
                                xt(ti), stat[:, 64 + ti:65 + ti],
                                stat[:, 72 + ti:73 + ti], ALU.mult, ALU.add))
                            m5_apply[ti] = s_dve.n
                    elif tt == 7 and _PHASES > 2:
                        batch_chain("ln", 4, 8)
                    if tt + 2 < NT:
                        sy.wait_ge(s_dve.sem, v_tile_done[tt])
                        i = sy.dma_start(
                            out=xstg[tt % 2][:, 0:2048],
                            in_=xin[(tt + 2) * 128:(tt + 3) * 128, :])
                        s_x.inc(i, 16)
                        xarr = s_x.n
        mlp2_guard = s_pe.n

    if _PHASES <= 2:
        xstg = [hT[:, i * 4096:(i + 1) * 4096].bitcast(F32) for i in range(2)]
        sy.wait_ge(s_pe.sem, h2T_free)
        vtd = [0] * NT
        for ti in range(NT):
            if ti >= 2:
                sy.wait_ge(s_dve.sem, vtd[ti - 2])
            i = sy.dma_start(out=xstg[ti % 2][:, 0:2048],
                             in_=xin[ti * 128:(ti + 1) * 128, :])
            s_x.inc(i, 16)
            ve.wait_ge(s_x.sem, s_x.n)
            i = ve.tensor_tensor(xt(ti), xstg[ti % 2][:, 0:2048], xt(ti), ALU.add)
            s_dve.inc(i)
            vtd[ti] = s_dve.n
        _epilogue(sy, s_pe, s_dve, s_act, s_st, out, x_res, dbg, dbg_srcs)
        return _finish(nc, sy, gp, te, ve, sc)

    m5_guard = [0, 0, 0, 0]
    for ti in range(4):
        stg = regA[:, 11264 + ti * 2048:11264 + (ti + 1) * 2048]
        for g4 in range(4):
            bk = pool.acquire(te, 1)[0]
            if g4 == 0:
                te.wait_ge(s_dve.sem, m5_apply[ti])
            pv = pb[bk].ap().bitcast(BF16)
            for j in range(4):
                dc = g4 * 4 + j
                i = te.transpose(pv[:, j * 128:(j + 1) * 128],
                                 stg[:, dc * 128:(dc + 1) * 128], idb[:])
            s_pe.inc(i)
            m5_guard[ti] = s_pe.n
            sc.wait_ge(s_pe.sem, s_pe.n)
            dst = hT_r[:, 4 * g4:4 * g4 + 4, ti * 128:(ti + 1) * 128]
            srcv = pv[:, 0:512].rearrange("p (c t) -> p c t", t=128)
            i = sc.copy(dst, srcv)
            s_act.inc(i)
            pool.release(bk, s_act)
    stage45 = [regA[:, 11264:13312], regA[:, 13312:15360]]
    sguard45 = [m5_guard[0], m5_guard[1]]
    for ti in range(4, 8):
        _apply_tile("ln", ti, stage45, sguard45)
    for tt in range(NT):
        bk = pool.acquire(te, 1)[0]
        if tt == 0:
            te.wait_ge(s_act.sem, s_act.n)
            te.wait_ge(s_r.sem, s_r.n)
        for dc in range(ND):
            i = te.matmul(pb[bk].ap()[:, 0:9],
                          hT[:, dc * 1024 + tt * 128: dc * 1024 + (tt + 1) * 128],
                          routw[:, dc * 9:(dc + 1) * 9],
                          start=(dc == 0), stop=(dc == ND - 1))
        s_pe.inc(i)
        sc.wait_ge(s_pe.sem, s_pe.n)
        sc.copy(smallA[:, tt * 8:tt * 8 + 8], pb[bk].ap()[:, 0:8])
        i = sc.copy(smallA[:, 64 + tt:65 + tt], pb[bk].ap()[:, 8:9])
        s_act.inc(i)
        pool.release(bk, s_act)
    logits_done = s_act.n
    def g3(ap_):
        return ap_.rearrange("p (t e) -> p t e", e=8)

    def b3(ap8):
        return ap8.rearrange("p (t e) -> p t e", e=1).broadcast_to([128, 8, 8])

    L3 = g3(smallA[:, 0:64])
    E = smallA[:, 128:192]
    E3 = g3(E)
    MSK = smallA[:, 192:256]
    M3v = g3(MSK)
    r1_8 = stat[:, 40:48]
    ssum8 = stat[:, 48:56]
    r1e8 = stat[:, 56:64]
    rinv8 = stat[:, 64:72]
    r2_8 = stat[:, 72:80]
    ve.wait_ge(s_act.sem, logits_done)
    V("tensor_reduce", g3(r1_8), L3, AX.X, ALU.max)
    V("tensor_tensor", E3, L3, b3(r1_8), ALU.subtract)
    S("activation", E, E, AF.Exp)
    ve.wait_ge(s_act.sem, s_act.n)
    V("tensor_reduce", g3(ssum8), E3, AX.X, ALU.add)
    V("tensor_reduce", g3(r1e8), E3, AX.X, ALU.max)
    V("reciprocal", rinv8, ssum8)
    V("tensor_tensor", M3v, E3, b3(r1e8), ALU.is_ge)
    V("tensor_scalar", MSK, MSK, -2.0, None, ALU.mult)
    V("tensor_tensor", MSK, E, MSK, ALU.add)
    V("tensor_reduce", g3(r2_8), M3v, AX.X, ALU.max)
    V("tensor_tensor", M3v, E3, b3(r2_8), ALU.is_ge)
    V("tensor_tensor", E, E, MSK, ALU.mult)
    V("tensor_tensor", E3, E3, b3(rinv8), ALU.mult)
    co_done = s_dve.n
    S("activation", smallA[:, 72:80], smallA[:, 64:72], AF.Sigmoid)
    sig_done = s_act.n
    te.wait_ge(s_dve.sem, co_done)
    te.wait_ge(s_act.sem, sig_done)
    for tt in range(NT):
        bks = pool.acquire(te, 3)
        for j in range(9):
            bkj = bks[j // 4]
            cosrc = (E[:, tt * 8 + j:tt * 8 + j + 1] if j < 8
                     else smallA[:, 72 + tt:73 + tt])
            i = te.transpose(pb[bkj].ap()[0:1, (j % 4) * 128:(j % 4) * 128 + 128],
                             cosrc, idf[:])
            if j in (3, 7, 8):
                s_pe.inc(i)
                sc.wait_ge(s_pe.sem, s_pe.n)
                lo = (j // 4) * 4
                njj = j - lo + 1
                dstv = coT[0:1, :].rearrange("p (c t) -> p c t", t=1024)[
                    0:1, lo:lo + njj, tt * 128:(tt + 1) * 128]
                srcv = pb[bks[j // 4]].ap()[0:1, 0:njj * 128].rearrange(
                    "p (c t) -> p c t", t=128)
                ii = sc.copy(dstv, srcv)
                s_act.inc(ii)
                pool.release(bks[j // 4], s_act)
    coT_done = s_act.n
    bks = pool.acquire(te, 2)
    te.wait_ge(s_act.sem, coT_done)
    for g in range(2):
        i = te.matmul(pb[bks[g]].ap()[:, 0:512], ones[0:1, :],
                      coT[0:1, 8 * 1024 + g * 512: 8 * 1024 + (g + 1) * 512],
                      start=True, stop=True)
        s_pe.inc(i)
    sc.wait_ge(s_pe.sem, s_pe.n)
    sc.copy(sig_sb[:, 0:512], pb[bks[0]].ap()[:, 0:512])
    i = sc.copy(sig_sb[:, 512:1024], pb[bks[1]].ap()[:, 0:512])
    s_act.inc(i)
    pool.release(bks[0], s_act)
    pool.release(bks[1], s_act)

    ffn_tile_done = [0] * NT

    def make_bc(e):
        bks = pool.acquire(te, 2)
        if bc_state["guard"] > 0:
            te.wait_ge(s_dve.sem, bc_state["guard"])
        for g in range(2):
            i = te.matmul(pb[bks[g]].ap()[:, 0:512], ones[0:1, :],
                          coT[0:1, e * 1024 + g * 512: e * 1024 + (g + 1) * 512],
                          start=True, stop=True)
            s_pe.inc(i)
        sc.wait_ge(s_pe.sem, s_pe.n)
        if bc_state["guard"] > 0:
            sc.wait_ge(s_dve.sem, bc_state["guard"])
        sc.copy(bc_sb[:, 0:512], pb[bks[0]].ap()[:, 0:512])
        i = sc.copy(bc_sb[:, 512:1024], pb[bks[1]].ap()[:, 0:512])
        s_act.inc(i)
        pool.release(bks[0], s_act)
        pool.release(bks[1], s_act)

    bc_state = {"guard": 0}

    def gu_down_block(n_pairs, gu_src, down_src, gate_mul_sb, first_block,
                      pre_down=None, store_out=False):
        nonlocal ffn_tile_done
        down_pieces = [(0, 4), (4, 8), (8, 11)]
        down_ready = None
        hp_marks = []
        for j in range(n_pairs):
            bg = bu = None
            for half in range(2):
                ci = 2 * j + half
                slf = gu_down_block.fidx % 3
                gu_down_block.fidx += 1
                if ringF_guard[slf] > 0:
                    gp.wait_ge(s_pe.sem, ringF_guard[slf])
                i = gp.dma_start(out=ringF[slf][:], in_=gu_src[ci])
                s_f[slf].inc(i, 16)
                if (j, half) in ((2, 1), (4, 0), (5, 1)) and down_pieces:
                    lo, hi = down_pieces.pop(0)
                    if lo == 0:
                        gp.wait_ge(s_pe.sem, gu_down_block.down_guard)
                    i = gp.dma_start(
                        out=wbuf[:, lo * 2048:hi * 2048].rearrange(
                            "p (c n) -> p c n", n=2048),
                        in_=down_src[lo:hi].rearrange("c p n -> p c n"))
                    s_r.inc(i, 16)
                    down_ready = s_r.n
                te.wait_ge(s_f[slf].sem, s_f[slf].n)
                if first_block and j == 0 and half == 0:
                    te.wait_ge(s_act.sem, s_act.n)
                bks = pool.acquire(te, 2)
                for g in range(2):
                    for dc in range(ND):
                        st0, sp0 = dc == 0, dc == ND - 1
                        lhs = ringF[slf][:, dc * 128:(dc + 1) * 128]
                        i = te.matmul(pb[bks[g]].ap()[:, 0:512], lhs, ht_chunk(dc, g),
                                      start=st0, stop=sp0)
                s_pe.inc(i)
                ringF_guard[slf] = s_pe.n
                if half == 0:
                    bg, g_mm = bks, s_pe.n
                else:
                    bu, u_mm = bks, s_pe.n
            sc.wait_ge(s_pe.sem, g_mm)
            if gu_down_block.tmpA_guard > 0:
                sc.wait_ge(s_dve.sem, gu_down_block.tmpA_guard)
            sc.activation(tmpA[:, 0:512], pb[bg[0]].ap()[:, 0:512], AF.Silu)
            i = sc.activation(tmpA[:, 512:1024], pb[bg[1]].ap()[:, 0:512], AF.Silu)
            s_act.inc(i)
            gsilu = s_act.n
            pool.release(bg[0], s_act)
            pool.release(bg[1], s_act)
            ve.wait_ge(s_act.sem, gsilu)
            ve.wait_ge(s_pe.sem, u_mm)
            if gu_down_block.hp_guard > 0 and j == 0:
                ve.wait_ge(s_pe.sem, gu_down_block.hp_guard)
            ve.tensor_tensor(smallA[:, 0:512], tmpA[:, 0:512],
                             pb[bu[0]].ap()[:, 0:512], ALU.mult)
            i = ve.tensor_tensor(smallA[:, 512:1024], tmpA[:, 512:1024],
                                 pb[bu[1]].ap()[:, 0:512], ALU.mult)
            s_dve.inc(i)
            gu_down_block.tmpA_guard = s_dve.n
            pool.release(bu[0], s_dve)
            pool.release(bu[1], s_dve)
            dve_self(ve)
            i = ve.tensor_tensor(ya(j), smallA[:, 0:1024], gate_mul_sb, ALU.mult)
            s_dve.inc(i)
            hp_marks.append(s_dve.n)
        hp_mark = hp_marks[-1]
        te.wait_ge(s_dve.sem, hp_marks[-2])
        if pre_down is not None:
            pre_down()
        te.wait_ge(s_r.sem, down_ready)
        for tt in range(NT):
            for dh in range(2):
                bks = pool.acquire(te, 2)
                for hf in range(2):
                    for ic in range(11):
                        st0, sp0 = ic == 0, ic == 10
                        if tt == 0 and dh == 0 and ic == 10:
                            te.wait_ge(s_dve.sem, hp_marks[-1])
                        lhs = ya(ic)[:, tt * 128:(tt + 1) * 128]
                        off = ic * 2048 + dh * 1024 + hf * 512
                        i = te.matmul(pb[bks[hf]].ap()[:, 0:512], lhs,
                                      wbuf[:, off:off + 512], start=st0, stop=sp0)
                s_pe.inc(i)
                ve.wait_ge(s_pe.sem, s_pe.n)
                base = tt * 2048 + dh * 1024
                xs0 = x_res[:, base:base + 512]
                xs1 = x_res[:, base + 512:base + 1024]
                ve.tensor_tensor(xs0, pb[bks[0]].ap()[:, 0:512], xs0, ALU.add)
                i = ve.tensor_tensor(xs1, pb[bks[1]].ap()[:, 0:512], xs1, ALU.add)
                s_dve.inc(i)
                pool.release(bks[0], s_dve)
                pool.release(bks[1], s_dve)
                ffn_tile_done[tt] = s_dve.n
                if store_out and dh == 1:
                    sy.wait_ge(s_dve.sem, s_dve.n)
                    i = sy.dma_start(out=out[tt * 128:(tt + 1) * 128, :],
                                     in_=xt(tt))
                    s_st.inc(i, 16)
        gu_down_block.hp_guard = s_pe.n
        gu_down_block.down_guard = s_pe.n

    gu_down_block.fidx = 0
    gu_down_block.tmpA_guard = 0
    gu_down_block.hp_guard = 0
    gu_down_block.down_guard = mlp2_guard

    te.wait_ge(s_act.sem, coT_done)
    make_bc(0)
    for e in range(8):
        def _pre(e=e):
            bc_state["guard"] = s_dve.n
            if e < 7:
                make_bc(e + 1)
        gu_down_block(11, d_egu[e], d_ed[e], bc_sb[:, 0:1024], e == 0,
                      pre_down=_pre)
    for q in range(4):
        gu_down_block(11, d_sgu[q], d_sd[q], sig_sb[:, 0:1024], False,
                      store_out=(q == 3))

    sy.wait_ge(s_pe.sem, s_pe.n)
    sy.wait_ge(s_dve.sem, s_dve.n)
    sy.wait_ge(s_act.sem, s_act.n)
    if dbg:
        for k, dst in dbg.items():
            i = sy.dma_start(out=dst, in_=dbg_srcs[k])
            s_st.inc(i, 16)
    sy.wait_ge(s_st.sem, s_st.n)
    return _finish(nc, sy, gp, te, ve, sc)


def _epilogue(sy, s_pe, s_dve, s_act, s_st, out, x_res, dbg=None, dbg_srcs=None):
    sy.wait_ge(s_pe.sem, s_pe.n)
    sy.wait_ge(s_dve.sem, s_dve.n)
    sy.wait_ge(s_act.sem, s_act.n)
    i = sy.dma_start(out=out.rearrange("(i p) n -> p i n", p=128),
                     in_=x_res[:].rearrange("p (i n) -> p i n", n=2048))
    s_st.inc(i, 16)
    if dbg:
        for k, dst in dbg.items():
            i = sy.dma_start(out=dst, in_=dbg_srcs[k])
            s_st.inc(i, 16)
    sy.wait_ge(s_st.sem, s_st.n)


def _finish(nc, sy, gp, te, ve, sc):
    with nc.Block() as block:
        def replay(eng, proxy):
            for attr, a, k, res in proxy.q:
                inst = getattr(eng, attr)(*a, **k)
                for sem, by in res.post:
                    inst.then_inc(sem, by)

        @block.sync
        def _(e):
            replay(e, sy)

        @block.gpsimd
        def _(e):
            replay(e, gp)

        @block.tensor
        def _(e):
            replay(e, te)

        @block.vector
        def _(e):
            replay(e, ve)

        @block.scalar
        def _(e):
            replay(e, sc)
    return nc


def kernel(**inputs):
    blobs = _pack(inputs)
    x = np.ascontiguousarray(np.asarray(inputs["L0_attn_x"], np.float32)
                             .reshape(NCORES * TPC, D))
    in_maps = []
    for c in range(NCORES):
        m = {"xc": np.ascontiguousarray(x[c * TPC:(c + 1) * TPC])}
        if c % 2 == 1:
            m["xh"] = np.ascontiguousarray(x[c * TPC - 3:c * TPC])
        else:
            m["xh"] = np.zeros((3, D), np.float32)
        m.update({"b_win": blobs["win"], "b_wout": blobs["wout"],
                  "b_m1": blobs["m1"], "b_m2": blobs["m2"],
                  "b_egu": blobs["egu"], "b_ed": blobs["ed"],
                  "b_sgu": blobs["sgu"], "b_sd": blobs["sd"],
                  "b_convw": blobs["convw"], "b_convb": blobs["convb"],
                  "b_rout": blobs["rout"], "b_idf": blobs["idf"],
                  "b_idb": blobs["idb"], "b_ones": blobs["ones"]})
        in_maps.append(m)
    nc = build_program()
    res = run_bass_kernel_spmd(nc, in_maps, core_ids=list(range(NCORES)))
    outp = np.concatenate([np.asarray(r["out"]) for r in res.results], axis=0)
    if _DEBUG:
        kernel.last_results = res.results
    return outp.reshape(4, 2048, 2048)


# version 17 (final)
# speedup vs baseline: 1.0112x; 1.0112x over previous
import numpy as np
import ml_dtypes
import concourse.bass as bass
import concourse.mybir as mybir
from concourse.bass_utils import run_bass_kernel_spmd

F32 = mybir.dt.float32
BF16 = mybir.dt.bfloat16
AF = mybir.ActivationFunctionType
ALU = mybir.AluOpType
AX = mybir.AxisListType

NCORES = 8
D = 2048
TPC = 1024
NT = 8
ND = 16
EPS = 1e-5
SQD = float(np.sqrt(D))

_PHASES = 99
_DEBUG = False
_M2N = 8

BASE = 16512
OFF_X = BASE
OFF_HT = BASE + 65536
OFF_A = BASE + 98304
OFF_W = BASE + 163840
OFF_C = BASE + 208896
A_RING = OFF_A + 40960
A_TMP = OFF_A + 53248
A_BC = OFF_A + 57344
A_SIG = OFF_A + 59392
A_SMALL = OFF_A + 61440
W_SCR = OFF_W + 16384
C_IDB = OFF_C
C_CONVW = OFF_C + 256
C_CONVB = OFF_C + 768
C_ROUT = OFF_C + 896
C_ONES = OFF_C + 1184
C_IDF = OFF_C + 1440
C_STAT = OFF_C + 1952
C_HALOT = OFF_C + 2336


class Ctr:
    def __init__(self, nc, name):
        self.sem = nc.alloc_semaphore(name)
        self.n = 0

    def inc(self, inst, by=1):
        inst.then_inc(self.sem, by)
        self.n += by
        return self.n


class InstProxy:
    __slots__ = ("post",)

    def __init__(self):
        self.post = []

    def then_inc(self, sem, by):
        self.post.append((sem, by))
        return self


class EngProxy:
    def __init__(self):
        self.q = []

    def __getattr__(self, attr):
        def call(*a, **k):
            res = InstProxy()
            self.q.append((attr, a, k, res))
            return res
        return call


class BankPool:

    def __init__(self, pb):
        self.pb = pb
        self.guard = [None] * len(pb)
        self.idx = 0

    def acquire(self, te, n=1):
        out = []
        for _ in range(n):
            i = self.idx
            self.idx = (self.idx + 1) % len(self.pb)
            g = self.guard[i]
            if g is not None and g[1] > 0:
                te.wait_ge(g[0].sem, g[1])
            self.guard[i] = None
            out.append(i)
        return out

    def release(self, i, ctr):
        self.guard[i] = (ctr, ctr.n)


def _pack(inputs):
    f = np.ascontiguousarray
    W_in = np.asarray(inputs["L0_attn_W_in"], np.float32)
    conv_w = np.asarray(inputs["L0_attn_conv_w"], np.float32)
    conv_b = np.asarray(inputs["L0_attn_conv_b"], np.float32)
    W_out = np.asarray(inputs["L0_attn_W_out"], np.float32)
    W_m1 = np.asarray(inputs["L0_attn_W_mlp1"], np.float32)
    W_m2 = np.asarray(inputs["L0_attn_W_mlp2"], np.float32)
    W_r = np.asarray(inputs["L0_ffn_W_router"], np.float32)
    W_sg = np.asarray(inputs["L0_ffn_W_sgate"], np.float32)
    Wsgu = np.asarray(inputs["L0_ffn_Ws_gate_up"], np.float32)
    Wsd = np.asarray(inputs["L0_ffn_Ws_down"], np.float32)
    Wegu = np.asarray(inputs["L0_ffn_We_gate_up"], np.float32)
    Wed = np.asarray(inputs["L0_ffn_We_down"], np.float32)

    b = {}
    t = W_in.reshape(64, 128, 16, 128).transpose(0, 3, 2, 1)
    seq = []
    for c in range(32):
        seq += [c, c + 32]
    b["win"] = f(t[seq].reshape(64, 128, 2048))
    b["wout"] = f(W_out.T.reshape(32, 128, 8, 256).transpose(2, 0, 1, 3))
    b["m1"] = f(W_m1.reshape(64, 128, 16, 128).transpose(0, 3, 2, 1)
                .reshape(64, 128, 2048))
    b["m2"] = f(W_m2.T.reshape(8, 8, 128, 2048))
    seqe = []
    for c in range(11):
        seqe += [c, c + 11]
    t = Wegu.reshape(8, 16, 128, 22, 128).transpose(0, 3, 2, 1, 4)
    b["egu"] = f(t[:, seqe].reshape(8, 22, 128, 2048))
    b["ed"] = f(Wed.reshape(8, 11, 128, 2048))
    t = Wsgu.reshape(16, 128, 88, 128).transpose(2, 1, 0, 3)
    sq = []
    for q in range(4):
        for j in range(11):
            sq += [11 * q + j, 44 + 11 * q + j]
    b["sgu"] = f(t[np.array(sq)].reshape(4, 22, 128, 2048))
    b["sd"] = f(Wsd.reshape(4, 11, 128, 2048))
    b["convw"] = f(conv_w.reshape(32, 128, 4).transpose(1, 0, 2).reshape(128, 128))
    b["convb"] = f(conv_b.reshape(32, 128).T)
    rout = np.zeros((128, 144), np.float32)
    for dc in range(16):
        rout[:, dc * 9:dc * 9 + 8] = W_r[:, dc * 128:(dc + 1) * 128].T
        rout[:, dc * 9 + 8] = W_sg[0, dc * 128:(dc + 1) * 128]
    b["rout"] = f(rout)
    b["idf"] = np.eye(128, dtype=np.float32)
    b["idb"] = np.eye(128, dtype=np.float32).astype(ml_dtypes.bfloat16)
    b["ones"] = np.ones((1, 128), ml_dtypes.bfloat16)
    return b


def build_program():
    nc = bass.Bass("TRN2", target_bir_lowering=False)

    xin = nc.dram_tensor("xc", [TPC, D], F32, kind="ExternalInput").ap()
    xhal = nc.dram_tensor("xh", [3, D], F32, kind="ExternalInput").ap()
    d_win = nc.dram_tensor("b_win", [64, 128, 2048], F32, kind="ExternalInput").ap()
    d_wout = nc.dram_tensor("b_wout", [8, 32, 128, 256], F32, kind="ExternalInput").ap()
    d_m1 = nc.dram_tensor("b_m1", [64, 128, 2048], F32, kind="ExternalInput").ap()
    d_m2 = nc.dram_tensor("b_m2", [8, 8, 128, 2048], F32, kind="ExternalInput").ap()
    d_egu = nc.dram_tensor("b_egu", [8, 22, 128, 2048], F32, kind="ExternalInput").ap()
    d_ed = nc.dram_tensor("b_ed", [8, 11, 128, 2048], F32, kind="ExternalInput").ap()
    d_sgu = nc.dram_tensor("b_sgu", [4, 22, 128, 2048], F32, kind="ExternalInput").ap()
    d_sd = nc.dram_tensor("b_sd", [4, 11, 128, 2048], F32, kind="ExternalInput").ap()
    d_convw = nc.dram_tensor("b_convw", [128, 128], F32, kind="ExternalInput").ap()
    d_convb = nc.dram_tensor("b_convb", [128, 32], F32, kind="ExternalInput").ap()
    d_rout = nc.dram_tensor("b_rout", [128, 144], F32, kind="ExternalInput").ap()
    d_idf = nc.dram_tensor("b_idf", [128, 128], F32, kind="ExternalInput").ap()
    d_idb = nc.dram_tensor("b_idb", [128, 128], BF16, kind="ExternalInput").ap()
    d_ones = nc.dram_tensor("b_ones", [1, 128], BF16, kind="ExternalInput").ap()
    out = nc.dram_tensor("out", [TPC, D], F32, kind="ExternalOutput").ap()
    dbg = {}
    if _DEBUG:
        dbg["hT"] = nc.dram_tensor("dbg_hT", [128, ND * 1024], BF16, kind="ExternalOutput").ap()
        dbg["haloT"] = nc.dram_tensor("dbg_haloT", [128, 48], BF16, kind="ExternalOutput").ap()
        dbg["y"] = nc.dram_tensor("dbg_y", [128, 32 * 1024], BF16, kind="ExternalOutput").ap()
        dbg["xp"] = nc.dram_tensor("dbg_xp", [128, 1027], F32, kind="ExternalOutput").ap()
        dbg["sxc"] = nc.dram_tensor("dbg_sxc", [128, 2048], BF16, kind="ExternalOutput").ap()
        dbg["woutb"] = nc.dram_tensor("dbg_woutb", [128, 16384], BF16, kind="ExternalOutput").ap()

    def at(name, off, shape, dt):
        return nc.alloc_sbuf_tensor_at(name, shape, dt, offset=off).ap()

    x_res = at("x_res", OFF_X, [128, NT * 2048], F32)
    hT = at("hT", OFF_HT, [128, ND * 1024], BF16)
    regA = at("regA", OFF_A, [128, 32 * 1024], BF16)
    coT = at("coT", OFF_A + 22528, [1, 9 * 1024], BF16)
    ringF = [at(f"ringF{i}", A_RING + i * 4096, [128, 2048], BF16) for i in range(3)]
    tmpA = at("tmpA", A_TMP, [128, 1024], F32)
    bc_sb = at("bc_sb", A_BC, [128, 1024], BF16)
    sig_sb = at("sig_sb", A_SIG, [128, 1024], BF16)
    smallA = at("smallA", A_SMALL, [128, 1024], F32)
    wbuf = at("wbuf", OFF_W, [128, 22528], BF16)
    ringW = [at(f"ringW{i}", OFF_W + i * 8192, [128, 4096], BF16) for i in range(2)]
    woutb = [at(f"woutb{i}", OFF_W + i * 16384, [128, 8192], BF16) for i in range(2)]
    xh_sb = at("xh_sb", W_SCR, [3, 2048], F32)
    xp_sb = at("xp_sb", W_SCR, [128, 1027], F32)
    tmp2 = at("tmp2", W_SCR + 4128, [128, 1024], F32)
    tmp3 = at("tmp3", W_SCR + 8224, [128, 1024], F32)
    sxc = at("sxc", W_SCR + 12320, [128, 2 * 1024], BF16)
    idb = at("idb", C_IDB, [128, 128], BF16)
    convw = at("convw", C_CONVW, [128, 128], F32)
    convb = at("convb", C_CONVB, [128, 32], F32)
    routw = at("routw", C_ROUT, [128, 144], BF16)
    ones = at("ones", C_ONES, [1, 128], BF16)
    idf = at("idf", C_IDF, [128, 128], F32)
    stat = at("stat", C_STAT, [128, 96], F32)
    haloT = at("haloT", C_HALOT, [128, 48], BF16)

    pb = [nc.alloc_psum_tensor(f"pb{i}", [128, 512], F32) for i in range(8)]

    s_x = Ctr(nc, "s_x")
    s_w = [Ctr(nc, "s_w0"), Ctr(nc, "s_w1")]
    s_f = [Ctr(nc, "s_f0"), Ctr(nc, "s_f1"), Ctr(nc, "s_f2")]
    s_r = Ctr(nc, "s_r")
    s_pool = Ctr(nc, "s_pool")
    s_xp = [Ctr(nc, "s_xp0"), Ctr(nc, "s_xp1")]
    s_pe = Ctr(nc, "s_pe")
    s_dve = Ctr(nc, "s_dve")
    s_act = Ctr(nc, "s_act")
    s_st = Ctr(nc, "s_st")

    sy, gp, te, ve, sc = EngProxy(), EngProxy(), EngProxy(), EngProxy(), EngProxy()
    pool = BankPool(pb)

    hT_r = hT[:].rearrange("p (dc t) -> p dc t", t=1024)

    def ht_chunk(dc, g=None):
        if g is None:
            return hT[:, dc * 1024:(dc + 1) * 1024]
        return hT[:, dc * 1024 + g * 512: dc * 1024 + (g + 1) * 512]

    def xt(ti):
        return x_res[:, ti * 2048:(ti + 1) * 2048]

    def ya(c):
        return regA[:, c * 1024:(c + 1) * 1024]

    def dve_self(v):
        v.wait_ge(s_dve.sem, s_dve.n)

    def V(op, *a, **k):
        dve_self(ve)
        s_dve.inc(getattr(ve, op)(*a, **k))

    def S(op, *a, **k):
        sc.wait_ge(s_act.sem, s_act.n)
        sc.wait_ge(s_dve.sem, s_dve.n)
        s_act.inc(getattr(sc, op)(*a, **k))

    def batch_stats(first_wait=None, tile_waits=None):
        for ti in range(NT):
            if first_wait is not None and ti == 0:
                ve.wait_ge(first_wait[0].sem, first_wait[1])
            if tile_waits is not None:
                ve.wait_ge(tile_waits[ti][0].sem, tile_waits[ti][1])
            dve_self(ve)
            for sgi in range(4):
                i = ve.bn_stats(stat[:, sgi * 6:sgi * 6 + 6],
                                xt(ti)[:, sgi * 512:(sgi + 1) * 512])
            s_dve.inc(i)
            dve_self(ve)
            s_dve.inc(ve.bn_aggr(stat[:, 24 + ti:24 + ti + 9:8], stat[:, 0:24]))

    def tile_stats(ti):
        dve_self(ve)
        for sgi in range(4):
            i = ve.bn_stats(stat[:, sgi * 6:sgi * 6 + 6],
                            xt(ti)[:, sgi * 512:(sgi + 1) * 512])
        s_dve.inc(i)
        dve_self(ve)
        s_dve.inc(ve.bn_aggr(stat[:, 24 + ti:24 + ti + 9:8], stat[:, 0:24]))

    def _apply_tile(kind, ti, stage, stage_guard):
        dve_self(ve)
        if stage_guard[ti % 2] > 0:
            ve.wait_ge(s_pe.sem, stage_guard[ti % 2])
        if kind == "ln_u":
            s_dve.inc(ve.tensor_scalar(
                xt(ti), xt(ti), stat[:, 64 + ti:65 + ti],
                stat[:, 72 + ti:73 + ti], ALU.mult, ALU.add))
            dve_self(ve)
            s_dve.inc(ve.tensor_scalar(
                stage[ti % 2], xt(ti), stat[:, 80 + ti:81 + ti], None,
                ALU.mult))
        else:
            s_dve.inc(ve.tensor_scalar(
                stage[ti % 2], xt(ti), stat[:, 64 + ti:65 + ti],
                stat[:, 72 + ti:73 + ti], ALU.mult, ALU.add))
        happly = s_dve.n
        for g4 in range(4):
            bk = pool.acquire(te, 1)[0]
            if g4 == 0:
                te.wait_ge(s_dve.sem, happly)
            pv = pb[bk].ap().bitcast(BF16)
            for j in range(4):
                dc = g4 * 4 + j
                i = te.transpose(pv[:, j * 128:(j + 1) * 128],
                                 stage[ti % 2][:, dc * 128:(dc + 1) * 128],
                                 idb[:])
            s_pe.inc(i)
            stage_guard[ti % 2] = s_pe.n
            sc.wait_ge(s_pe.sem, s_pe.n)
            dst = hT_r[:, 4 * g4:4 * g4 + 4, ti * 128:(ti + 1) * 128]
            srcv = pv[:, 0:512].rearrange("p (c t) -> p c t", t=128)
            i = sc.copy(dst, srcv)
            s_act.inc(i)
            pool.release(bk, s_act)

    def batch_chain(kind, lo=0, hi=8):
        m8 = stat[:, 24 + lo:24 + hi]
        v8 = stat[:, 32 + lo:32 + hi]
        c0 = stat[:, 40 + lo:40 + hi]
        c1 = stat[:, 48 + lo:48 + hi]
        c2 = stat[:, 56 + lo:56 + hi]
        s8 = stat[:, 64 + lo:64 + hi]
        b8 = stat[:, 72 + lo:72 + hi]
        r8 = stat[:, 80 + lo:80 + hi]
        if kind == "ln_u":
            V("tensor_scalar", c0, v8, EPS, None, ALU.add)
            V("tensor_scalar", c1, v8, float(D), 1e-30, ALU.mult, ALU.max)
            S("activation", c0, c0, AF.Ln)
            S("activation", c0, c0, AF.Exp, scale=-0.5)
            S("activation", c1, c1, AF.Sqrt)
            ve.wait_ge(s_act.sem, s_act.n)
            V("tensor_tensor", c2, c1, c0, ALU.mult)
            V("tensor_scalar", c2, c2, 1e-12, None, ALU.max)
            V("reciprocal", c2, c2)
            V("tensor_scalar", r8, c2, SQD, None, ALU.mult)
            V("tensor_copy", s8, c0)
            V("tensor_tensor", b8, m8, c0, ALU.mult)
            V("tensor_scalar", b8, b8, -1.0, None, ALU.mult)
        elif kind == "rms":
            V("tensor_tensor", c0, m8, m8, ALU.mult)
            V("tensor_tensor", c0, c0, v8, ALU.add)
            V("tensor_scalar", c0, c0, float(D), 1e-30, ALU.mult, ALU.max)
            S("activation", c2, c0, AF.Sqrt)
            ve.wait_ge(s_act.sem, s_act.n)
            V("tensor_scalar", c2, c2, 1e-12, None, ALU.max)
            V("reciprocal", s8, c2)
            V("tensor_scalar", s8, s8, SQD, None, ALU.mult)
            V("tensor_scalar", b8, s8, 0.0, None, ALU.mult)
        else:
            V("tensor_scalar", c0, v8, EPS, None, ALU.add)
            S("activation", c1, c0, AF.Ln)
            S("activation", s8, c1, AF.Exp, scale=-0.5)
            ve.wait_ge(s_act.sem, s_act.n)
            V("tensor_tensor", b8, m8, s8, ALU.mult)
            V("tensor_scalar", b8, b8, -1.0, None, ALU.mult)

    def norm_tile(kind, src_ap, s_col, b_col, npart=128):
        dve_self(ve)
        for sgi in range(4):
            i = ve.bn_stats(stat[0:npart, sgi * 6:sgi * 6 + 6],
                            src_ap[0:npart, sgi * 512:(sgi + 1) * 512])
        s_dve.inc(i)
        V("bn_aggr", stat[0:npart, 24:26], stat[0:npart, 0:24])
        m = stat[0:npart, 24:25]
        var = stat[0:npart, 25:26]
        c0 = stat[0:npart, 42:43]
        c1 = stat[0:npart, 43:44]
        c2 = stat[0:npart, 44:45]
        sA = stat[0:npart, s_col:s_col + 1]
        bA = stat[0:npart, b_col:b_col + 1]
        if kind == "ln":
            V("tensor_scalar", c0, var, EPS, None, ALU.add)
            S("activation", c1, c0, AF.Ln)
            S("activation", sA, c1, AF.Exp, scale=-0.5)
            ve.wait_ge(s_act.sem, s_act.n)
            V("tensor_tensor", bA, m, sA, ALU.mult)
            V("tensor_scalar", bA, bA, -1.0, None, ALU.mult)
        elif kind == "rms":
            V("tensor_tensor", c0, m, m, ALU.mult)
            V("tensor_tensor", c0, c0, var, ALU.add)
            V("tensor_scalar", c0, c0, float(D), 1e-30, ALU.mult, ALU.max)
            S("activation", c2, c0, AF.Sqrt)
            ve.wait_ge(s_act.sem, s_act.n)
            V("tensor_scalar", c2, c2, 1e-12, None, ALU.max)
            V("reciprocal", sA, c2)
            V("tensor_scalar", sA, sA, SQD, None, ALU.mult)
            V("tensor_scalar", bA, sA, 0.0, None, ALU.mult)
        elif kind == "ln_u":
            rA = stat[0:npart, 35:36]
            V("tensor_scalar", c0, var, EPS, None, ALU.add)
            V("tensor_scalar", c1, var, float(D), 1e-30, ALU.mult, ALU.max)
            S("activation", c0, c0, AF.Ln)
            S("activation", c0, c0, AF.Exp, scale=-0.5)
            S("activation", c1, c1, AF.Sqrt)
            ve.wait_ge(s_act.sem, s_act.n)
            V("tensor_tensor", c2, c1, c0, ALU.mult)
            V("tensor_scalar", c2, c2, 1e-12, None, ALU.max)
            V("reciprocal", c2, c2)
            V("tensor_scalar", rA, c2, SQD, None, ALU.mult)
            V("tensor_copy", sA, c0)
            V("tensor_tensor", bA, m, c0, ALU.mult)
            V("tensor_scalar", bA, bA, -1.0, None, ALU.mult)
        else:
            V("tensor_scalar", c0, var, EPS, None, ALU.add)
            V("tensor_scalar", c1, var, float(D), 1e-30, ALU.mult, ALU.max)
            S("activation", c0, c0, AF.Ln)
            S("activation", c0, c0, AF.Exp, scale=-0.5)
            S("activation", c1, c1, AF.Sqrt)
            ve.wait_ge(s_act.sem, s_act.n)
            V("tensor_tensor", c2, c1, c0, ALU.mult)
            V("tensor_scalar", c2, c2, 1e-12, None, ALU.max)
            V("reciprocal", c2, c2)
            V("tensor_scalar", c2, c2, SQD, None, ALU.mult)
            V("tensor_tensor", sA, c0, c2, ALU.mult)
            V("tensor_tensor", bA, m, sA, ALU.mult)
            V("tensor_scalar", bA, bA, -1.0, None, ALU.mult)

    def build_hT(kind, first_wait=None, stats_done=False, tile_waits=None,
                 stage_off=0, chain_done=False):
        def _emit_half(kind, tis):
            for ti in tis:
                _apply_tile(kind, ti, stage, stage_guard)

        """Norm + transpose x_res -> hT (all 8 tiles). Uses regA[0:4096] as
        bf16 staging (2 alternating tiles). kind 'ln_u' also rewrites
        x_res tile in place to ln(x)."""
        stage = [regA[:, stage_off:stage_off + 2048],
                 regA[:, stage_off + 2048:stage_off + 4096]]
        stage_guard = [0, 0]
        if not stats_done:
            for ti in range(NT):
                if first_wait is not None and ti == 0:
                    ve.wait_ge(first_wait[0].sem, first_wait[1])
                if tile_waits is not None:
                    ve.wait_ge(tile_waits[ti][0].sem, tile_waits[ti][1])
                dve_self(ve)
                for sgi in range(4):
                    i = ve.bn_stats(stat[:, sgi * 6:sgi * 6 + 6],
                                    xt(ti)[:, sgi * 512:(sgi + 1) * 512])
                s_dve.inc(i)
                dve_self(ve)
                s_dve.inc(ve.bn_aggr(stat[:, 24 + ti:24 + ti + 9:8],
                                     stat[:, 0:24]))
                if ti == 3:
                    batch_chain(kind, 0, 4)
                    _emit_half(kind, range(0, 4))
                elif ti == 7:
                    batch_chain(kind, 4, 8)
                    _emit_half(kind, range(4, 8))
            return
        if not chain_done:
            batch_chain(kind)
        for ti in range(NT):
            _apply_tile(kind, ti, stage, stage_guard)

    for src, dst in ((d_convw, convw[:]), (d_convb, convb[:]),
                     (d_idf, idf[:]), (d_idb, idb[:]), (d_ones, ones[:])):
        i = sy.dma_start(out=dst, in_=src)
        s_x.inc(i, 16)
    i = sy.dma_start(out=xh_sb[:], in_=xhal)
    s_x.inc(i, 16)
    consts_cnt = s_x.n
    xp_sems = [s_xp[0], s_xp[1], s_f[0], s_f[1]]
    x_piece = []
    for h in range(4):
        i = sy.dma_start(
            out=x_res[:, h * 4096:(h + 1) * 4096].rearrange(
                "p (i n) -> p i n", n=2048),
            in_=xin[h * 256:(h + 1) * 256, :].rearrange("(i p) n -> p i n", p=128))
        xp_sems[h].inc(i, 16)
        x_piece += [(xp_sems[h], xp_sems[h].n)] * 2

    i = gp.dma_start(out=routw[:], in_=d_rout)
    s_r.inc(i, 16)

    ve.wait_ge(s_x.sem, consts_cnt)
    norm_tile("lnrms", xh_sb, 26, 34, npart=3)
    hstage = smallA[:].bitcast(BF16)
    dve_self(ve)
    s_dve.inc(ve.tensor_scalar(hstage[0:3, 0:2048], xh_sb[:], stat[0:3, 26:27],
                               stat[0:3, 34:35], ALU.mult, ALU.add))
    bk = pool.acquire(te, 1)[0]
    te.wait_ge(s_dve.sem, s_dve.n)
    pv = pb[bk].ap().bitcast(BF16)
    for dc in range(ND):
        i = te.transpose(pv[:, dc * 4:dc * 4 + 3],
                         hstage[0:3, dc * 128:(dc + 1) * 128], idb[0:3, 0:3])
    s_pe.inc(i)
    sc.wait_ge(s_pe.sem, s_pe.n)
    i = sc.copy(haloT[:].rearrange("p (c t) -> p c t", t=3),
                       pv[:, 0:64].rearrange("p (c t) -> p c t", t=4)[:, :, 0:3])
    s_act.inc(i)
    pool.release(bk, s_act)
    build_hT("ln_u", tile_waits=x_piece)
    hT1_done = (s_act, s_act.n)

    dbg_srcs = {"hT": hT[:], "haloT": haloT[:], "y": regA[:], "xp": xp_sb[:], "sxc": sxc[:],
                "woutb": wbuf[:, 0:16384]}
    if _PHASES <= 0:
        _epilogue(sy, s_pe, s_dve, s_act, s_st, out, x_res, dbg, dbg_srcs)
        return _finish(nc, sy, gp, te, ve, sc)

    ringW_guard = [0, 0]
    convdone = 0
    tmpA_free = 0
    sxc_guard = [0, 0]
    y_done = []
    for p in range(32):
        sl = p % 2
        if ringW_guard[sl] > 0:
            gp.wait_ge(s_pe.sem, ringW_guard[sl])
        i = gp.dma_start(out=ringW[sl][:].rearrange("p (c n) -> p c n", c=2),
                         in_=d_win.rearrange("(g c) p n -> g p c n", c=2)[p])
        s_w[sl].inc(i, 16)
        te.wait_ge(s_w[sl].sem, s_w[sl].n)
        if p == 0:
            te.wait_ge(hT1_done[0].sem, hT1_done[1])
        bxp = pool.acquire(te, 2)
        bh = pool.acquire(te, 1)[0]
        for g in range(2):
            for dc in range(ND):
                st0, sp0 = dc == 0, dc == ND - 1
                lhs = ringW[sl][:, dc * 128:(dc + 1) * 128]
                i = te.matmul(pb[bxp[g]].ap()[:, 0:512], lhs, ht_chunk(dc, g),
                              start=st0, stop=sp0)
        for dc in range(ND):
            st0, sp0 = dc == 0, dc == ND - 1
            lhs = ringW[sl][:, dc * 128:(dc + 1) * 128]
            i = te.matmul(pb[bh].ap()[:, 0:3], lhs, haloT[:, dc * 3:(dc + 1) * 3],
                          start=st0, stop=sp0)
        s_pe.inc(i)
        xp_mm = s_pe.n
        bz = pool.acquire(te, 2)
        for g in range(2):
            for dc in range(ND):
                st0, sp0 = dc == 0, dc == ND - 1
                lhs = ringW[sl][:, 2048 + dc * 128:2048 + (dc + 1) * 128]
                i = te.matmul(pb[bz[g]].ap()[:, 0:512], lhs, ht_chunk(dc, g),
                              start=st0, stop=sp0)
        s_pe.inc(i)
        z_mm = s_pe.n
        ringW_guard[sl] = z_mm
        sc.wait_ge(s_pe.sem, xp_mm)
        if convdone > 0:
            sc.wait_ge(s_dve.sem, convdone)
        sc.copy(xp_sb[:, 3:515], pb[bxp[0]].ap()[:, 0:512])
        sc.copy(xp_sb[:, 515:1027], pb[bxp[1]].ap()[:, 0:512])
        i = sc.copy(xp_sb[:, 0:3], pb[bh].ap()[:, 0:3])
        s_act.inc(i)
        xp_evac = s_act.n
        pool.release(bxp[0], s_act)
        pool.release(bxp[1], s_act)
        pool.release(bh, s_act)
        ve.wait_ge(s_act.sem, xp_evac)
        ve.tensor_scalar(tmp2[:], xp_sb[:, 0:1024], convw[:, 4 * p:4 * p + 1],
                         convb[:, p:p + 1], ALU.mult, ALU.add)
        for k in (1, 2, 3):
            ve.tensor_scalar(tmp3[:], xp_sb[:, k:k + 1024],
                             convw[:, 4 * p + k:4 * p + k + 1], None, ALU.mult)
            i = ve.tensor_tensor(tmp2[:], tmp2[:], tmp3[:], ALU.add)
        s_dve.inc(i)
        convdone = s_dve.n
        sc.wait_ge(s_dve.sem, convdone)
        if sxc_guard[sl] > 0:
            sc.wait_ge(s_dve.sem, sxc_guard[sl])
        i = sc.activation(sxc[:, sl * 1024:(sl + 1) * 1024], tmp2[:], AF.Silu)
        s_act.inc(i)
        sxc_done = s_act.n
        sc.wait_ge(s_pe.sem, z_mm)
        if tmpA_free > 0:
            sc.wait_ge(s_dve.sem, tmpA_free)
        sc.activation(tmp3[:, 0:512], pb[bz[0]].ap()[:, 0:512], AF.Silu)
        i = sc.activation(tmp3[:, 512:1024], pb[bz[1]].ap()[:, 0:512], AF.Silu)
        s_act.inc(i)
        z_silu = s_act.n
        pool.release(bz[0], s_act)
        pool.release(bz[1], s_act)
        ve.wait_ge(s_act.sem, z_silu)
        i = ve.tensor_tensor(ya(p), tmp3[:], sxc[:, sl * 1024:(sl + 1) * 1024],
                             ALU.mult)
        s_dve.inc(i)
        tmpA_free = s_dve.n
        sxc_guard[sl] = s_dve.n
        y_done.append(s_dve.n)

    woutb_guard = [0, 0]
    m2_tile_done = [0] * NT
    for de in range(_M2N):
        sl = de % 2
        if de == 1:
            gp.wait_ge(s_dve.sem, y_done[31])
        if de == 0:
            gp.wait_ge(s_pe.sem, ringW_guard[0])
            i = gp.dma_start(
                out=woutb[0][:, 0:4096].rearrange("p (c n) -> p c n", n=256),
                in_=d_wout[0, 0:16].rearrange("c p n -> p c n"))
            s_w[0].inc(i, 16)
            gp.wait_ge(s_pe.sem, ringW_guard[1])
            i = gp.dma_start(
                out=woutb[0][:, 4096:8192].rearrange("p (c n) -> p c n", n=256),
                in_=d_wout[0, 16:32].rearrange("c p n -> p c n"))
            s_w[0].inc(i, 16)
        else:
            if woutb_guard[sl] > 0:
                gp.wait_ge(s_pe.sem, woutb_guard[sl])
            i = gp.dma_start(out=woutb[sl][:].rearrange("p (c n) -> p c n", n=256),
                             in_=d_wout[de].rearrange("c p n -> p c n"))
            s_w[sl].inc(i, 16)
        te.wait_ge(s_w[sl].sem, s_w[sl].n)
        if de == 0:
            te.wait_ge(s_dve.sem, y_done[31])
        for tt in range(NT):
            bk = pool.acquire(te, 1)[0]
            for ec in range(32):
                i = te.matmul(pb[bk].ap()[:, 0:256],
                              ya(ec)[:, tt * 128:(tt + 1) * 128],
                              woutb[sl][:, ec * 256:(ec + 1) * 256],
                              start=(ec == 0), stop=(ec == 31))
            s_pe.inc(i)
            ve.wait_ge(s_pe.sem, s_pe.n)
            xs = x_res[:, tt * 2048 + de * 256: tt * 2048 + (de + 1) * 256]
            i = ve.tensor_tensor(xs, pb[bk].ap()[:, 0:256], xs, ALU.add)
            s_dve.inc(i)
            pool.release(bk, s_dve)
            m2_tile_done[tt] = s_dve.n
            if de == 7 and _PHASES > 1:
                tile_stats(tt)
        woutb_guard[sl] = s_pe.n
    m2_last_mm = s_pe.n

    if _PHASES <= 1:
        _epilogue(sy, s_pe, s_dve, s_act, s_st, out, x_res, dbg, dbg_srcs)
        return _finish(nc, sy, gp, te, ve, sc)

    build_hT("rms", stats_done=True)

    ringF_guard = [m2_last_mm] * 3
    m5_apply = [0] * NT
    mlp2_guard = 0
    m4_tile_done = [0] * NT
    for q in range(8):
        for ec in range(8):
            gec = 8 * q + ec
            slf = gec % 3
            if ringF_guard[slf] > 0:
                gp.wait_ge(s_pe.sem, ringF_guard[slf])
            i = gp.dma_start(out=ringF[slf][:], in_=d_m1[gec])
            s_f[slf].inc(i, 16)
            if ec in (3, 5, 6, 7):
                k = (3, 5, 6, 7).index(ec)
                if k == 0:
                    gp.wait_ge(s_pe.sem, mlp2_guard if q > 0 else m2_last_mm)
                i = gp.dma_start(
                    out=wbuf[:, k * 4096:(k + 1) * 4096].rearrange(
                        "p (c n) -> p c n", n=2048),
                    in_=d_m2[q, 2 * k:2 * k + 2].rearrange("c p n -> p c n"))
                s_r.inc(i, 16)
            te.wait_ge(s_f[slf].sem, s_f[slf].n)
            if q == 0 and ec == 0:
                te.wait_ge(s_act.sem, s_act.n)
            bks = pool.acquire(te, 2)
            for g in range(2):
                for dc in range(ND):
                    st0, sp0 = dc == 0, dc == ND - 1
                    lhs = ringF[slf][:, dc * 128:(dc + 1) * 128]
                    i = te.matmul(pb[bks[g]].ap()[:, 0:512], lhs, ht_chunk(dc, g),
                                  start=st0, stop=sp0)
            s_pe.inc(i)
            ringF_guard[slf] = s_pe.n
            sc.wait_ge(s_pe.sem, s_pe.n)
            if mlp2_guard > 0 and ec == 0:
                sc.wait_ge(s_pe.sem, mlp2_guard)
            sc.activation(ya(ec)[:, 0:512], pb[bks[0]].ap()[:, 0:512], AF.Relu)
            i = sc.activation(ya(ec)[:, 512:1024], pb[bks[1]].ap()[:, 0:512], AF.Relu)
            s_act.inc(i)
            pool.release(bks[0], s_act)
            pool.release(bks[1], s_act)
        relu_all = s_act.n
        if q == 7:
            h2T_free = s_pe.n
            xstg = [hT[:, i * 4096:(i + 1) * 4096].bitcast(F32) for i in range(2)]
            v_tile_done = [0] * NT
            sy.wait_ge(s_pe.sem, h2T_free)
            for ti in range(2):
                i = sy.dma_start(out=xstg[ti % 2][:, 0:2048],
                                 in_=xin[ti * 128:(ti + 1) * 128, :])
                s_x.inc(i, 16)
                xarr = s_x.n
        te.wait_ge(s_r.sem, s_r.n)
        te.wait_ge(s_act.sem, relu_all)
        for tt in range(NT):
            for dh in range(2):
                bks = pool.acquire(te, 2)
                for hf in range(2):
                    for ec in range(8):
                        st0, sp0 = ec == 0, ec == 7
                        lhs = ya(ec)[:, tt * 128:(tt + 1) * 128]
                        off = ec * 2048 + dh * 1024 + hf * 512
                        i = te.matmul(pb[bks[hf]].ap()[:, 0:512], lhs,
                                      wbuf[:, off:off + 512], start=st0, stop=sp0)
                s_pe.inc(i)
                ve.wait_ge(s_pe.sem, s_pe.n)
                base = tt * 2048 + dh * 1024
                xs0 = x_res[:, base:base + 512]
                xs1 = x_res[:, base + 512:base + 1024]
                ve.tensor_tensor(xs0, pb[bks[0]].ap()[:, 0:512], xs0, ALU.add)
                i = ve.tensor_tensor(xs1, pb[bks[1]].ap()[:, 0:512], xs1, ALU.add)
                s_dve.inc(i)
                pool.release(bks[0], s_dve)
                pool.release(bks[1], s_dve)
                m4_tile_done[tt] = s_dve.n
                if q == 7 and dh == 1 and _PHASES > 2:
                    gp.wait_ge(s_x.sem, xarr)
                    gp.wait_ge(s_dve.sem, m4_tile_done[tt])
                    i = gp.tensor_tensor(xt(tt), xstg[tt % 2][:, 0:2048],
                                         xt(tt), ALU.add)
                    s_pool.inc(i)
                    v_tile_done[tt] = s_pool.n
                    ve.wait_ge(s_pool.sem, s_pool.n)
                    tile_stats(tt)
                    if tt == 3 and _PHASES > 2:
                        batch_chain("ln", 0, 4)
                        for ti in range(4):
                            dve_self(ve)
                            s_dve.inc(ve.tensor_scalar(
                                regA[:, 11264 + ti * 2048:11264 + (ti + 1) * 2048],
                                xt(ti), stat[:, 64 + ti:65 + ti],
                                stat[:, 72 + ti:73 + ti], ALU.mult, ALU.add))
                            m5_apply[ti] = s_dve.n
                    elif tt == 7 and _PHASES > 2:
                        batch_chain("ln", 4, 8)
                    if tt + 2 < NT:
                        sy.wait_ge(s_pool.sem, v_tile_done[tt])
                        i = sy.dma_start(
                            out=xstg[tt % 2][:, 0:2048],
                            in_=xin[(tt + 2) * 128:(tt + 3) * 128, :])
                        s_x.inc(i, 16)
                        xarr = s_x.n
        mlp2_guard = s_pe.n

    if _PHASES <= 2:
        xstg = [hT[:, i * 4096:(i + 1) * 4096].bitcast(F32) for i in range(2)]
        sy.wait_ge(s_pe.sem, h2T_free)
        vtd = [0] * NT
        for ti in range(NT):
            if ti >= 2:
                sy.wait_ge(s_dve.sem, vtd[ti - 2])
            i = sy.dma_start(out=xstg[ti % 2][:, 0:2048],
                             in_=xin[ti * 128:(ti + 1) * 128, :])
            s_x.inc(i, 16)
            ve.wait_ge(s_x.sem, s_x.n)
            i = ve.tensor_tensor(xt(ti), xstg[ti % 2][:, 0:2048], xt(ti), ALU.add)
            s_dve.inc(i)
            vtd[ti] = s_dve.n
        _epilogue(sy, s_pe, s_dve, s_act, s_st, out, x_res, dbg, dbg_srcs)
        return _finish(nc, sy, gp, te, ve, sc)

    m5_guard = [0, 0, 0, 0]
    for ti in range(4):
        stg = regA[:, 11264 + ti * 2048:11264 + (ti + 1) * 2048]
        for g4 in range(4):
            bk = pool.acquire(te, 1)[0]
            if g4 == 0:
                te.wait_ge(s_dve.sem, m5_apply[ti])
            pv = pb[bk].ap().bitcast(BF16)
            for j in range(4):
                dc = g4 * 4 + j
                i = te.transpose(pv[:, j * 128:(j + 1) * 128],
                                 stg[:, dc * 128:(dc + 1) * 128], idb[:])
            s_pe.inc(i)
            m5_guard[ti] = s_pe.n
            sc.wait_ge(s_pe.sem, s_pe.n)
            dst = hT_r[:, 4 * g4:4 * g4 + 4, ti * 128:(ti + 1) * 128]
            srcv = pv[:, 0:512].rearrange("p (c t) -> p c t", t=128)
            i = sc.copy(dst, srcv)
            s_act.inc(i)
            pool.release(bk, s_act)
    stage45 = [regA[:, 11264:13312], regA[:, 13312:15360]]
    sguard45 = [m5_guard[0], m5_guard[1]]
    for ti in range(4, 8):
        _apply_tile("ln", ti, stage45, sguard45)
    for tt in range(NT):
        bk = pool.acquire(te, 1)[0]
        if tt == 0:
            te.wait_ge(s_act.sem, s_act.n)
            te.wait_ge(s_r.sem, s_r.n)
        for dc in range(ND):
            i = te.matmul(pb[bk].ap()[:, 0:9],
                          hT[:, dc * 1024 + tt * 128: dc * 1024 + (tt + 1) * 128],
                          routw[:, dc * 9:(dc + 1) * 9],
                          start=(dc == 0), stop=(dc == ND - 1))
        s_pe.inc(i)
        sc.wait_ge(s_pe.sem, s_pe.n)
        sc.copy(smallA[:, tt * 8:tt * 8 + 8], pb[bk].ap()[:, 0:8])
        i = sc.copy(smallA[:, 64 + tt:65 + tt], pb[bk].ap()[:, 8:9])
        s_act.inc(i)
        pool.release(bk, s_act)
    logits_done = s_act.n
    def g3(ap_):
        return ap_.rearrange("p (t e) -> p t e", e=8)

    def b3(ap8):
        return ap8.rearrange("p (t e) -> p t e", e=1).broadcast_to([128, 8, 8])

    L3 = g3(smallA[:, 0:64])
    E = smallA[:, 128:192]
    E3 = g3(E)
    MSK = smallA[:, 192:256]
    M3v = g3(MSK)
    r1_8 = stat[:, 40:48]
    ssum8 = stat[:, 48:56]
    r1e8 = stat[:, 56:64]
    rinv8 = stat[:, 64:72]
    r2_8 = stat[:, 72:80]
    ve.wait_ge(s_act.sem, logits_done)
    V("tensor_reduce", g3(r1_8), L3, AX.X, ALU.max)
    V("tensor_tensor", E3, L3, b3(r1_8), ALU.subtract)
    S("activation", E, E, AF.Exp)
    ve.wait_ge(s_act.sem, s_act.n)
    V("tensor_reduce", g3(ssum8), E3, AX.X, ALU.add)
    V("tensor_reduce", g3(r1e8), E3, AX.X, ALU.max)
    V("reciprocal", rinv8, ssum8)
    V("tensor_tensor", M3v, E3, b3(r1e8), ALU.is_ge)
    V("tensor_scalar", MSK, MSK, -2.0, None, ALU.mult)
    V("tensor_tensor", MSK, E, MSK, ALU.add)
    V("tensor_reduce", g3(r2_8), M3v, AX.X, ALU.max)
    V("tensor_tensor", M3v, E3, b3(r2_8), ALU.is_ge)
    V("tensor_tensor", E, E, MSK, ALU.mult)
    V("tensor_tensor", E3, E3, b3(rinv8), ALU.mult)
    co_done = s_dve.n
    S("activation", smallA[:, 72:80], smallA[:, 64:72], AF.Sigmoid)
    sig_done = s_act.n
    te.wait_ge(s_dve.sem, co_done)
    te.wait_ge(s_act.sem, sig_done)
    for tt in range(NT):
        bks = pool.acquire(te, 3)
        for j in range(9):
            bkj = bks[j // 4]
            cosrc = (E[:, tt * 8 + j:tt * 8 + j + 1] if j < 8
                     else smallA[:, 72 + tt:73 + tt])
            i = te.transpose(pb[bkj].ap()[0:1, (j % 4) * 128:(j % 4) * 128 + 128],
                             cosrc, idf[:])
            if j in (3, 7, 8):
                s_pe.inc(i)
                sc.wait_ge(s_pe.sem, s_pe.n)
                lo = (j // 4) * 4
                njj = j - lo + 1
                dstv = coT[0:1, :].rearrange("p (c t) -> p c t", t=1024)[
                    0:1, lo:lo + njj, tt * 128:(tt + 1) * 128]
                srcv = pb[bks[j // 4]].ap()[0:1, 0:njj * 128].rearrange(
                    "p (c t) -> p c t", t=128)
                ii = sc.copy(dstv, srcv)
                s_act.inc(ii)
                pool.release(bks[j // 4], s_act)
    coT_done = s_act.n
    bks = pool.acquire(te, 2)
    te.wait_ge(s_act.sem, coT_done)
    for g in range(2):
        i = te.matmul(pb[bks[g]].ap()[:, 0:512], ones[0:1, :],
                      coT[0:1, 8 * 1024 + g * 512: 8 * 1024 + (g + 1) * 512],
                      start=True, stop=True)
        s_pe.inc(i)
    sc.wait_ge(s_pe.sem, s_pe.n)
    sc.copy(sig_sb[:, 0:512], pb[bks[0]].ap()[:, 0:512])
    i = sc.copy(sig_sb[:, 512:1024], pb[bks[1]].ap()[:, 0:512])
    s_act.inc(i)
    pool.release(bks[0], s_act)
    pool.release(bks[1], s_act)

    ffn_tile_done = [0] * NT

    def make_bc(e):
        bks = pool.acquire(te, 2)
        if bc_state["guard"] > 0:
            te.wait_ge(s_dve.sem, bc_state["guard"])
        for g in range(2):
            i = te.matmul(pb[bks[g]].ap()[:, 0:512], ones[0:1, :],
                          coT[0:1, e * 1024 + g * 512: e * 1024 + (g + 1) * 512],
                          start=True, stop=True)
            s_pe.inc(i)
        sc.wait_ge(s_pe.sem, s_pe.n)
        if bc_state["guard"] > 0:
            sc.wait_ge(s_dve.sem, bc_state["guard"])
        sc.copy(bc_sb[:, 0:512], pb[bks[0]].ap()[:, 0:512])
        i = sc.copy(bc_sb[:, 512:1024], pb[bks[1]].ap()[:, 0:512])
        s_act.inc(i)
        pool.release(bks[0], s_act)
        pool.release(bks[1], s_act)

    bc_state = {"guard": 0}

    def gu_down_block(n_pairs, gu_src, down_src, gate_mul_sb, first_block,
                      pre_down=None, store_out=False):
        nonlocal ffn_tile_done
        down_pieces = [(0, 4), (4, 8), (8, 11)]
        down_ready = None
        hp_marks = []
        for j in range(n_pairs):
            bg = bu = None
            for half in range(2):
                ci = 2 * j + half
                slf = gu_down_block.fidx % 3
                gu_down_block.fidx += 1
                if ringF_guard[slf] > 0:
                    gp.wait_ge(s_pe.sem, ringF_guard[slf])
                i = gp.dma_start(out=ringF[slf][:], in_=gu_src[ci])
                s_f[slf].inc(i, 16)
                if (j, half) in ((2, 1), (4, 0), (5, 1)) and down_pieces:
                    lo, hi = down_pieces.pop(0)
                    if lo == 0:
                        gp.wait_ge(s_pe.sem, gu_down_block.down_guard)
                    i = gp.dma_start(
                        out=wbuf[:, lo * 2048:hi * 2048].rearrange(
                            "p (c n) -> p c n", n=2048),
                        in_=down_src[lo:hi].rearrange("c p n -> p c n"))
                    s_r.inc(i, 16)
                    down_ready = s_r.n
                te.wait_ge(s_f[slf].sem, s_f[slf].n)
                if first_block and j == 0 and half == 0:
                    te.wait_ge(s_act.sem, s_act.n)
                bks = pool.acquire(te, 2)
                for g in range(2):
                    for dc in range(ND):
                        st0, sp0 = dc == 0, dc == ND - 1
                        lhs = ringF[slf][:, dc * 128:(dc + 1) * 128]
                        i = te.matmul(pb[bks[g]].ap()[:, 0:512], lhs, ht_chunk(dc, g),
                                      start=st0, stop=sp0)
                s_pe.inc(i)
                ringF_guard[slf] = s_pe.n
                if half == 0:
                    bg, g_mm = bks, s_pe.n
                else:
                    bu, u_mm = bks, s_pe.n
            sc.wait_ge(s_pe.sem, g_mm)
            if gu_down_block.tmpA_guard > 0:
                sc.wait_ge(s_dve.sem, gu_down_block.tmpA_guard)
            sc.activation(tmpA[:, 0:512], pb[bg[0]].ap()[:, 0:512], AF.Silu)
            i = sc.activation(tmpA[:, 512:1024], pb[bg[1]].ap()[:, 0:512], AF.Silu)
            s_act.inc(i)
            gsilu = s_act.n
            pool.release(bg[0], s_act)
            pool.release(bg[1], s_act)
            ve.wait_ge(s_act.sem, gsilu)
            ve.wait_ge(s_pe.sem, u_mm)
            if gu_down_block.hp_guard > 0 and j == 0:
                ve.wait_ge(s_pe.sem, gu_down_block.hp_guard)
            ve.tensor_tensor(smallA[:, 0:512], tmpA[:, 0:512],
                             pb[bu[0]].ap()[:, 0:512], ALU.mult)
            i = ve.tensor_tensor(smallA[:, 512:1024], tmpA[:, 512:1024],
                                 pb[bu[1]].ap()[:, 0:512], ALU.mult)
            s_dve.inc(i)
            gu_down_block.tmpA_guard = s_dve.n
            pool.release(bu[0], s_dve)
            pool.release(bu[1], s_dve)
            dve_self(ve)
            i = ve.tensor_tensor(ya(j), smallA[:, 0:1024], gate_mul_sb, ALU.mult)
            s_dve.inc(i)
            hp_marks.append(s_dve.n)
        hp_mark = hp_marks[-1]
        te.wait_ge(s_dve.sem, hp_marks[-2])
        if pre_down is not None:
            pre_down()
        te.wait_ge(s_r.sem, down_ready)
        for tt in range(NT):
            for dh in range(2):
                bks = pool.acquire(te, 2)
                for hf in range(2):
                    for ic in range(11):
                        st0, sp0 = ic == 0, ic == 10
                        if tt == 0 and dh == 0 and ic == 10:
                            te.wait_ge(s_dve.sem, hp_marks[-1])
                        lhs = ya(ic)[:, tt * 128:(tt + 1) * 128]
                        off = ic * 2048 + dh * 1024 + hf * 512
                        i = te.matmul(pb[bks[hf]].ap()[:, 0:512], lhs,
                                      wbuf[:, off:off + 512], start=st0, stop=sp0)
                s_pe.inc(i)
                ve.wait_ge(s_pe.sem, s_pe.n)
                base = tt * 2048 + dh * 1024
                xs0 = x_res[:, base:base + 512]
                xs1 = x_res[:, base + 512:base + 1024]
                ve.tensor_tensor(xs0, pb[bks[0]].ap()[:, 0:512], xs0, ALU.add)
                i = ve.tensor_tensor(xs1, pb[bks[1]].ap()[:, 0:512], xs1, ALU.add)
                s_dve.inc(i)
                pool.release(bks[0], s_dve)
                pool.release(bks[1], s_dve)
                ffn_tile_done[tt] = s_dve.n
                if store_out and dh == 1:
                    sy.wait_ge(s_dve.sem, s_dve.n)
                    i = sy.dma_start(out=out[tt * 128:(tt + 1) * 128, :],
                                     in_=xt(tt))
                    s_st.inc(i, 16)
        gu_down_block.hp_guard = s_pe.n
        gu_down_block.down_guard = s_pe.n

    gu_down_block.fidx = 0
    gu_down_block.tmpA_guard = 0
    gu_down_block.hp_guard = 0
    gu_down_block.down_guard = mlp2_guard

    te.wait_ge(s_act.sem, coT_done)
    make_bc(0)
    for e in range(8):
        def _pre(e=e):
            bc_state["guard"] = s_dve.n
            if e < 7:
                make_bc(e + 1)
        gu_down_block(11, d_egu[e], d_ed[e], bc_sb[:, 0:1024], e == 0,
                      pre_down=_pre)
    for q in range(4):
        gu_down_block(11, d_sgu[q], d_sd[q], sig_sb[:, 0:1024], False,
                      store_out=(q == 3))

    sy.wait_ge(s_pe.sem, s_pe.n)
    sy.wait_ge(s_dve.sem, s_dve.n)
    sy.wait_ge(s_act.sem, s_act.n)
    if s_pool.n > 0:
        sy.wait_ge(s_pool.sem, s_pool.n)
    if dbg:
        for k, dst in dbg.items():
            i = sy.dma_start(out=dst, in_=dbg_srcs[k])
            s_st.inc(i, 16)
    sy.wait_ge(s_st.sem, s_st.n)
    return _finish(nc, sy, gp, te, ve, sc)


def _epilogue(sy, s_pe, s_dve, s_act, s_st, out, x_res, dbg=None, dbg_srcs=None):
    sy.wait_ge(s_pe.sem, s_pe.n)
    sy.wait_ge(s_dve.sem, s_dve.n)
    sy.wait_ge(s_act.sem, s_act.n)
    i = sy.dma_start(out=out.rearrange("(i p) n -> p i n", p=128),
                     in_=x_res[:].rearrange("p (i n) -> p i n", n=2048))
    s_st.inc(i, 16)
    if dbg:
        for k, dst in dbg.items():
            i = sy.dma_start(out=dst, in_=dbg_srcs[k])
            s_st.inc(i, 16)
    sy.wait_ge(s_st.sem, s_st.n)


def _finish(nc, sy, gp, te, ve, sc):
    with nc.Block() as block:
        def replay(eng, proxy):
            for attr, a, k, res in proxy.q:
                inst = getattr(eng, attr)(*a, **k)
                for sem, by in res.post:
                    inst.then_inc(sem, by)

        @block.sync
        def _(e):
            replay(e, sy)

        @block.gpsimd
        def _(e):
            replay(e, gp)

        @block.tensor
        def _(e):
            replay(e, te)

        @block.vector
        def _(e):
            replay(e, ve)

        @block.scalar
        def _(e):
            replay(e, sc)
    return nc


def kernel(**inputs):
    blobs = _pack(inputs)
    x = np.ascontiguousarray(np.asarray(inputs["L0_attn_x"], np.float32)
                             .reshape(NCORES * TPC, D))
    in_maps = []
    for c in range(NCORES):
        m = {"xc": np.ascontiguousarray(x[c * TPC:(c + 1) * TPC])}
        if c % 2 == 1:
            m["xh"] = np.ascontiguousarray(x[c * TPC - 3:c * TPC])
        else:
            m["xh"] = np.zeros((3, D), np.float32)
        m.update({"b_win": blobs["win"], "b_wout": blobs["wout"],
                  "b_m1": blobs["m1"], "b_m2": blobs["m2"],
                  "b_egu": blobs["egu"], "b_ed": blobs["ed"],
                  "b_sgu": blobs["sgu"], "b_sd": blobs["sd"],
                  "b_convw": blobs["convw"], "b_convb": blobs["convb"],
                  "b_rout": blobs["rout"], "b_idf": blobs["idf"],
                  "b_idb": blobs["idb"], "b_ones": blobs["ones"]})
        in_maps.append(m)
    nc = build_program()
    res = run_bass_kernel_spmd(nc, in_maps, core_ids=list(range(NCORES)))
    outp = np.concatenate([np.asarray(r["out"]) for r in res.results], axis=0)
    if _DEBUG:
        kernel.last_results = res.results
    return outp.reshape(4, 2048, 2048)
